# version 07 (final)
# speedup vs baseline: 1.1853x; 1.0886x over previous
import numpy as np
import ml_dtypes
from contextlib import ExitStack

import concourse.bass as bass
import concourse.mybir as mybir
from concourse.bass_utils import run_bass_kernel_spmd
from concourse.alu_op_type import AluOpType as ALU

F32 = mybir.dt.float32
BF16 = mybir.dt.bfloat16
FP8 = mybir.dt.float8e5
AF = mybir.ActivationFunctionType

NCORE = 8
T = 16384
D = 1024
TC = T // NCORE
NT = TC // 128
DC = D // 128
H = 4096
HC = H // 128
NF = 512
CPC = NF // NCORE
D_GROUPS = 8
D_QUADS = 2
D_STAGE = 5
D_SUB = 9


class Ctx:
    def __init__(self):
        self.nc = bass.Bass("TRN2", target_bir_lowering=False, debug=False)
        self.es = ExitStack()
        self.cur_es = None
        nc = self.nc
        self.v, self.s, self.t, self.g, self.sy = (
            nc.vector, nc.scalar, nc.tensor, nc.gpsimd, nc.sync)
        self._sems = {}
        self.cnt = {}
        for s_ in ["ld", "gd", "sd", "cv", "cs", "ct", "cg", "st"]:
            self.sem(s_)

    def barrier(self):
        for eng in (self.t, self.s, self.v, self.g, self.sy):
            for nm, h in self._sems.items():
                if self.cnt[nm] and not nm.startswith("cc"):
                    eng.wait_ge(h, self.cnt[nm])

    def sem(self, name):
        h = self.es.enter_context(self.nc.semaphore(name))
        self._sems[name] = h
        self.cnt[name] = 0
        return h

    def inc(self, instr, name, n=1):
        instr.then_inc(self._sems[name], n)
        self.cnt[name] += n
        return self.cnt[name]

    def wait(self, eng, name, val=None):
        if val is None:
            val = self.cnt[name]
        eng.wait_ge(self._sems[name], val)

    def mile(self, name):
        eng = {"cv": self.v, "cs": self.s, "ct": self.t, "cg": self.g}[name]
        eng.wait_ge(self._sems[name], self.cnt[name])
        return self.cnt[name]

    def dma(self, queue, stream, out, in_, gates=()):
        if stream not in self._sems:
            self.sem(stream)
        for nm, val in gates:
            queue.wait_ge(self._sems[nm], val)
        queue.dma_start(out=out, in_=in_).then_inc(self._sems[stream], 16)
        self.cnt[stream] += 16
        return self.cnt[stream]

    def sb(self, name, shape, dtype, es=None):
        return (es or self.cur_es or self.es).enter_context(
            self.nc.sbuf_tensor(getattr(self, "prefix", "") + name,
                                list(shape), dtype))

    def ps(self, name, shape, dtype=F32, es=None):
        return (es or self.cur_es or self.es).enter_context(
            self.nc.psum_tensor(getattr(self, "prefix", "") + name,
                                list(shape), dtype))

    def din(self, name, shape, dtype):
        return self.nc.dram_tensor(name, list(shape), dtype, kind="ExternalInput").ap()

    def dout(self, name, shape, dtype):
        return self.nc.dram_tensor(name, list(shape), dtype, kind="ExternalOutput").ap()

    def dint(self, name, shape, dtype):
        return self.nc.dram_tensor(name, list(shape), dtype).ap()

    def load_idents(self):
        self.d_identbf = self.din("identbf", [128, 128], BF16)
        self.d_identf32 = self.din("identf32", [128, 128], F32)
        self.ident_bf = self.sb("identbf_sb", [128, 128], BF16)
        self.ident_f32 = self.sb("identf32_sb", [128, 128], F32)
        self.sy.dma_start(out=self.ident_bf[:], in_=self.d_identbf[:]).then_inc(
            self._sems["ld"], 16)
        self.sy.dma_start(out=self.ident_f32[:], in_=self.d_identf32[:]).then_inc(
            self._sems["ld"], 16)
        self.cnt["ld"] += 32
        self.ld_idents = self.cnt["ld"]

    def bias_col(self, dram_vec, n, name, psb, es=None, t_gates=(), dma_gates=()):
        m = n // 128
        row = self.sb(name + "_r", [m, 128], F32, es)
        col = self.sb(name, [128, m], F32, es)
        sem = "ldb_" + name
        self.dma(self.sy, sem, row[:], dram_vec.rearrange("(a b) -> a b", b=128),
                 gates=dma_gates)
        self.wait(self.t, sem)
        self.t.wait_ge(self._sems["ld"], self.ld_idents)
        if getattr(self, "_bias_prev_ev", 0):
            self.t.wait_ge(self._sems["cv"], self._bias_prev_ev)
        for nm_, val_ in t_gates:
            self.t.wait_ge(self._sems[nm_], val_)
        tok = self.inc(self.t.transpose(psb[0:128, 0:m], row[:], self.ident_f32[0:m, 0:m]), "ct")
        self.wait(self.v, "ct", tok)
        vtok = self.inc(self.v.tensor_copy(col[:], psb[0:128, 0:m]), "cv")
        self._bias_prev_ev = vtok
        return col, vtok


def phase_A(c, io):
    v, s, t, g, sy = c.v, c.s, c.t, c.g, c.sy
    c.prefix = "A_"
    c.cur_es = ExitStack()
    x = io["x"]
    w1 = io["w1"]
    w2 = io["w2"]
    w3 = io["w3"]
    w4 = io["w4"]
    b1 = io["b1"]
    b2 = io["b2"]
    b3 = io["b3"]
    fw1t = io["fw1t"]
    fw2t = io["fw2t"]
    fb1 = io["fb1"]
    fb2 = io["fb2"]
    qf = io["qf"]
    kf = io["kf"]
    vw4 = io["vw4"]
    y2o = io["y2o"]

    tp = [c.ps(f"tp{j}", [128, 128], BF16) for j in range(4)]
    pj = [c.ps(f"pj{j}", [128, 512], F32) for j in range(2)]
    pk = [c.ps(f"pk{j}", [128, 512], F32) for j in range(2)]

    brows = {}
    for nm, vec, n in [("b1c", b1, D), ("b2c", b2, D), ("b3c", b3, D),
                       ("fb1c", fb1, H), ("fb2c", fb2, D)]:
        m = n // 128
        row = c.sb(nm + "_r", [m, 128], F32)
        c.dma(sy, "ldc", row[:], vec.rearrange("(a b) -> a b", b=128))
        brows[nm] = (row, m)
    ldc_total = c.cnt["ldc"]

    bcols = {}
    t.wait_ge(c._sems["ldc"], ldc_total)
    t.wait_ge(c._sems["ld"], c.ld_idents)
    prev_ev = 0
    for nm, (row, m) in brows.items():
        col = c.sb(nm, [128, m], F32)
        t.wait_ge(c._sems["cv"], prev_ev)
        tok = c.inc(t.transpose(pj[0][0:128, 0:m], row[:], c.ident_f32[0:m, 0:m]),
                    "ct")
        c.wait(v, "ct", tok)
        vtok = c.inc(v.tensor_copy(col[:], pj[0][0:128, 0:m]), "cv")
        prev_ev = vtok
        bcols[nm] = (col, vtok)
    b1c, tok1 = bcols["b1c"]
    b2c, tok2 = bcols["b2c"]
    b3c, tok3 = bcols["b3c"]
    fb1c, tok4 = bcols["fb1c"]
    fb2c, tok5 = bcols["fb2c"]
    fb1c8 = c.sb("fb1c8", [128, HC], FP8)
    fb2c8 = c.sb("fb2c8", [128, DC], FP8)
    c.wait(v, "cv", tok4)
    tq1 = c.inc(v.tensor_copy(fb1c8[:], fb1c[:]), "cv")
    c.wait(v, "cv", tq1)
    c.inc(v.tensor_copy(fb1c[:], fb1c8[:]), "cv")
    c.wait(v, "cv", tok5)
    tq3 = c.inc(v.tensor_copy(fb2c8[:], fb2c[:]), "cv")
    c.wait(v, "cv", tq3)
    c.inc(v.tensor_copy(fb2c[:], fb2c8[:]), "cv")
    bias_ready = c.mile("cv")

    xq_fm = c.sb("xq_fm", [128, DC, TC], BF16)

    es2 = ExitStack()
    xn_fm = c.sb("xn_fm", [128, DC, TC], BF16, es2)
    v_fm = c.sb("v_fm", [128, DC, TC], BF16, es2)
    w_sb = [c.sb(f"w_sb{j}", [128, DC, D], BF16, es2) for j in range(2)]
    outst = [c.sb(f"outst{j}", [128, 512], BF16, es2) for j in range(2)]
    vwob = [c.sb(f"vwob{j}", [128, D], BF16, es2) for j in range(2)]

    es1 = ExitStack()
    xt = [c.sb(f"xt{j}", [128, D], F32, es1) for j in range(2)]
    scr = c.sb("scr", [128, D], F32, es1)
    scr8 = c.sb("scr8", [128, D], FP8, es1)
    xnb = c.sb("xnb", [128, D], BF16, es1)
    xqb = c.sb("xqb", [128, D], BF16, es1)
    sx = c.sb("sx", [128, 1], F32, es1)
    ssq = c.sb("ssq", [128, 1], F32, es1)
    mu = c.sb("mu", [128, 1], F32, es1)
    musq = c.sb("musq", [128, 1], F32, es1)
    var = c.sb("var", [128, 1], F32, es1)
    rstd = c.sb("rstd", [128, 1], F32, es1)
    eps = c.sb("eps", [128, 1], F32, es1)
    junk = c.sb("junk", [128, D], BF16, es1)

    c.inc(v.memset(eps[:], 1e-5), "cv")
    c.mile("cv")

    tile_done = [0] * NT
    tp_free = [0, 0, 0, 0]
    for i in range(NT):
        lxs = f"lx{i % 2}"
        gates = [("cv", tile_done[i - 2])] if i >= 2 else []
        ldt = c.dma(sy, lxs, xt[i % 2][:], x[i * 128:(i + 1) * 128, :], gates)
        xi = xt[i % 2]
        s.wait_ge(c._sems[lxs], ldt)
        t1 = c.inc(s.activation(out=junk[:], in_=xi[:], func=AF.Square,
                                accum_out=ssq[:]), "cs")
        v.wait_ge(c._sems[lxs], ldt)
        t2 = c.inc(v.tensor_scalar(out=scr[:], in0=xi[:], scalar1=0.0, scalar2=None,
                                   op0=ALU.add, op1=ALU.add, accum_out=sx[:]), "cv")
        c.wait(v, "cv", t2)
        t3 = c.inc(v.tensor_scalar(out=mu[:], in0=sx[:], scalar1=1.0 / D,
                                   scalar2=None, op0=ALU.mult), "cv")
        c.wait(v, "cv", t3)
        t4 = c.inc(v.tensor_tensor(out=musq[:], in0=mu[:], in1=mu[:], op=ALU.mult), "cv")
        c.wait(v, "cs", t1)
        c.wait(v, "cv", t4)
        t5 = c.inc(v.scalar_tensor_tensor(out=var[:], in0=ssq[:], scalar=1.0 / D,
                                          in1=musq[:], op0=ALU.mult,
                                          op1=ALU.subtract), "cv")
        c.wait(s, "cv", t5)
        t6 = c.inc(s.activation(out=rstd[:], in_=var[:], func=AF.Ln, bias=eps[:]), "cs")
        c.wait(s, "cs", t6)
        t7 = c.inc(s.activation(out=rstd[:], in_=rstd[:], func=AF.Exp, scale=-0.5), "cs")
        c.wait(v, "cs", t7)
        t8 = c.inc(v.tensor_scalar(out=scr[:], in0=xi[:], scalar1=mu[:], scalar2=rstd[:],
                                   op0=ALU.subtract, op1=ALU.mult), "cv")
        tile_done[i] = t8
        c.wait(v, "cv", t8)
        t9 = c.inc(v.tensor_copy(xnb[:], scr[:]), "cv")
        c.wait(v, "cv", t9)
        t10 = c.inc(v.tensor_copy(scr8[:], scr[:]), "cv")
        c.wait(v, "cv", t10)
        t11 = c.inc(v.tensor_copy(xqb[:], scr8[:]), "cv")
        for which, (src_, dst) in enumerate([(xnb, xn_fm), (xqb, xq_fm)]):
            for cc in range(DC):
                j = (which * DC + cc) % 4
                t.wait_ge(c._sems["cv"], tp_free[j])
                t.wait_ge(c._sems["cv"], t11)
                tt = c.inc(t.transpose(tp[j][:], src_[:, cc * 128:(cc + 1) * 128],
                                       c.ident_bf[:]), "ct")
                c.wait(v, "ct", tt)
                ev = c.inc(v.tensor_copy(dst[:, cc, i * 128:(i + 1) * 128], tp[j][:]),
                           "cv")
                tp_free[j] = ev
        if i < NT - 1:
            v.wait_ge(c._sems["ct"], c.cnt["ct"])
    fm_ready = c.mile("cv")
    es1.close()

    def load_w(dram_w, buf, stream, gate_ct=0):
        gates = [("ct", gate_ct)] if gate_ct else []
        return c.dma(g, stream, buf[:], dram_w.rearrange("(a p) n -> p a n", p=128),
                     gates)

    wtokA = load_w(w1, w_sb[0], "wd0")
    wtokB = load_w(w2, w_sb[1], "wd1")

    outst_free = [0, 0]

    def projection(wbuf, wsem, wtok_, bias, bias_tok, rhs_fm, dst_dram=None,
                   dst_fm=None):
        t.wait_ge(c._sems[wsem], wtok_)
        t.wait_ge(c._sems["cv"], max(fm_ready, bias_tok))
        pj_free = [c.cnt["cs"]] * 2
        for oc in range(DC):
            for qb in range(TC // 512):
                bank = (oc * (TC // 512) + qb) % 2
                t.wait_ge(c._sems["cs"], pj_free[bank])
                for dc in range(DC):
                    mm = t.matmul(pj[bank][:], wbuf[:, dc, oc * 128:(oc + 1) * 128],
                                  rhs_fm[:, dc, qb * 512:(qb + 1) * 512],
                                  start=(dc == 0), stop=(dc == DC - 1))
                mt = c.inc(mm, "ct")
                s.wait_ge(c._sems["ct"], mt)
                if dst_fm is not None:
                    ep = c.inc(s.activation(out=dst_fm[:, oc, qb * 512:(qb + 1) * 512],
                                            in_=pj[bank][:], func=AF.Identity,
                                            bias=bias[:, oc:oc + 1]), "cs")
                else:
                    sto = f"sto{bank}"
                    if sto in c._sems:
                        s.wait_ge(c._sems[sto], outst_free[bank])
                    ep = c.inc(s.activation(out=outst[bank][:], in_=pj[bank][:],
                                            func=AF.Identity,
                                            bias=bias[:, oc:oc + 1]), "cs")
                    outst_free[bank] = c.dma(
                        g, sto,
                        dst_dram[oc * 128:(oc + 1) * 128, qb * 512:(qb + 1) * 512],
                        outst[bank][:], gates=[("cs", ep)])
                pj_free[bank] = ep
        return c.cnt["cs"], c.cnt["ct"]

    _, qct = projection(w_sb[0], "wd0", wtokA, b1c, tok1, xn_fm, dst_dram=qf)
    wtokC = load_w(w3, w_sb[0], "wd0", gate_ct=qct)
    _, kct = projection(w_sb[1], "wd1", wtokB, b2c, tok2, xn_fm, dst_dram=kf)
    wtokD = load_w(w4, w_sb[1], "wd1", gate_ct=kct)
    io["_ag"](io["kf"], io["kf_g"], "cc0",
              gates=[("sto0", outst_free[0]), ("sto1", outst_free[1])])
    projection(w_sb[0], "wd0", wtokC, b3c, tok3, xn_fm, dst_fm=v_fm)

    t.wait_ge(c._sems["wd1"], wtokD)
    vw_free = [0, 0]
    pv_free = [c.cnt["cs"]] * 2
    for i in range(NT):
        bank = i % 2
        stv = f"stv{bank}"
        t.wait_ge(c._sems["cs"], pv_free[bank])
        for dc in range(DC):
            for half in range(2):
                mm = t.matmul((pk[bank] if half == 0 else pj[bank])[:],
                              v_fm[:, dc, i * 128:(i + 1) * 128],
                              w_sb[1][:, dc, half * 512:(half + 1) * 512],
                              start=(dc == 0), stop=(dc == DC - 1))
                if dc == DC - 1:
                    c.inc(mm, "ct")
        mt = c.cnt["ct"]
        s.wait_ge(c._sems["ct"], mt)
        if stv in c._sems:
            s.wait_ge(c._sems[stv], vw_free[bank])
        c.inc(s.activation(out=vwob[bank][:, 0:512], in_=pk[bank][:], func=AF.Copy),
              "cs")
        ep = c.inc(s.activation(out=vwob[bank][:, 512:1024], in_=pj[bank][:],
                                func=AF.Copy), "cs")
        pv_free[bank] = ep
        vw_free[bank] = c.dma(g, stv, vw4[i * 128:(i + 1) * 128, :], vwob[bank][:],
                              gates=[("cs", ep)])

    io["_ag"](io["vw4"], io["vw4_g"], "cc1",
              gates=[("stv0", vw_free[0]), ("stv1", vw_free[1])])

    es2.close()
    w1q = c.sb("w1q", [128, DC, H], FP8)
    w2q = c.sb("w2q", [128, HC, D], FP8)
    h_bf = c.sb("h_bf", [128, HC, 512], BF16)
    h_q8 = c.sb("h_q8", [128, HC, 512], FP8)
    y2fm = c.sb("y2fm", [128, DC, 512], BF16)
    y2tm = [c.sb(f"y2tm{j}", [128, D], BF16) for j in range(2)]
    wq_gates = [("ct", c.cnt["ct"]), ("sto0", outst_free[0]), ("sto1", outst_free[1]),
                ("stv0", vw_free[0]), ("stv1", vw_free[1]), ("cs", c.cnt["cs"])]
    c.dma(g, "wq", w1q[:], fw1t.rearrange("(a p) n -> p a n", p=128), gates=wq_gates)
    c.dma(g, "wq", w2q[:], fw2t.rearrange("(a p) n -> p a n", p=128))
    wq_tok = c.cnt["wq"]

    t.wait_ge(c._sems["wq"], wq_tok)
    s.wait_ge(c._sems["cv"], bias_ready)
    prev_h8_done = 0
    prev_y2tp_ct = 0
    y2tm_free = [0, 0]
    for qb in range(TC // 512):
        pq_free = [c.cnt["cs"]] * 2
        for hc in range(HC):
            bank = hc % 2
            t.wait_ge(c._sems["cs"], pq_free[bank])
            for dc in range(DC):
                mm = t.matmul(pj[bank][:], w1q[:, dc, hc * 128:(hc + 1) * 128],
                              xq_fm[:, dc, qb * 512:(qb + 1) * 512],
                              start=(dc == 0), stop=(dc == DC - 1))
            mt = c.inc(mm, "ct")
            s.wait_ge(c._sems["ct"], mt)
            s.wait_ge(c._sems["cv"], prev_h8_done)
            ep = c.inc(s.activation(out=h_bf[:, hc, :], in_=pj[bank][:], func=AF.Relu,
                                    bias=fb1c[:, hc:hc + 1]), "cs")
            pq_free[bank] = ep
            v.wait_ge(c._sems["cs"], ep)
            c.inc(v.tensor_copy(h_q8[:, hc, :], h_bf[:, hc, :]), "cv")
        h8_done = c.mile("cv")
        t.wait_ge(c._sems["cv"], h8_done)
        py_free = [c.cnt["cs"]] * 2
        for oc in range(DC):
            bank = oc % 2
            t.wait_ge(c._sems["cs"], py_free[bank])
            for hc in range(HC):
                mm = t.matmul(pk[bank][:], w2q[:, hc, oc * 128:(oc + 1) * 128],
                              h_q8[:, hc, :], start=(hc == 0), stop=(hc == HC - 1))
            mt = c.inc(mm, "ct")
            s.wait_ge(c._sems["ct"], mt)
            s.wait_ge(c._sems["ct"], prev_y2tp_ct)
            ep = c.inc(s.activation(out=y2fm[:, oc, :], in_=pk[bank][:],
                                    func=AF.Identity, bias=fb2c[:, oc:oc + 1]), "cs")
            py_free[bank] = ep
        y2_ready = c.mile("cs")
        for tsub in range(4):
            jb = tsub % 2
            sty = f"sty{jb}"
            for oc in range(DC):
                j = (tsub * DC + oc) % 4
                t.wait_ge(c._sems["cv"], tp_free[j])
                t.wait_ge(c._sems["cs"], y2_ready)
                tt = c.inc(t.transpose(tp[j][:],
                                       y2fm[:, oc, tsub * 128:(tsub + 1) * 128],
                                       c.ident_bf[:]), "ct")
                c.wait(v, "ct", tt)
                if oc == 0 and sty in c._sems:
                    v.wait_ge(c._sems[sty], y2tm_free[jb])
                ev = c.inc(v.tensor_copy(y2tm[jb][:, oc * 128:(oc + 1) * 128],
                                         tp[j][:]), "cv")
                tp_free[j] = ev
            ti = qb * 4 + tsub
            y2tm_free[jb] = c.dma(g, sty, y2o[ti * 128:(ti + 1) * 128, :],
                                  y2tm[jb][:], gates=[("cv", c.cnt["cv"])])
        prev_h8_done = h8_done
        prev_y2tp_ct = c.cnt["ct"]
    for nm in ["sto0", "sto1", "stv0", "stv1", "sty0", "sty1"]:
        if nm in c._sems:
            g.wait_ge(c._sems[nm], c.cnt[nm])
    c.cur_es.close()
    c.cur_es = None


def phase_B(c, io):
    v, s, t, g, sy = c.v, c.s, c.t, c.g, c.sy
    c.prefix = "B_"
    c.cur_es = ExitStack()
    x = io["x"]
    qfull = io["qf"]
    kfull_g = io["kf_g"]
    vw4full = io["vw4_g"]
    y2 = io["y2o"]
    mtm = io["mtm"]
    b4 = io["b4"]
    x1o = io["x1d"]

    KCH = 1024
    NKC = T // KCH
    KTPC = KCH // 128

    ss = [c.ps(f"ss{j}", [128, 256], F32) for j in range(2)]
    su = [c.ps(f"su{j}", [128, 1], F32) for j in range(2)]
    op = [[c.ps(f"op{q}{h}", [128, 512], F32) for h in range(2)] for q in range(2)]

    qsb = c.sb("qsb", [128, DC, TC], BF16)
    onesc = c.sb("onesc", [128, 1], BF16)
    onesr = c.sb("onesr", [1, 128], F32)
    b4rep = c.sb("b4rep", [128, D], F32)
    b4row = c.sb("b4row", [1, D], F32)
    c.dma(sy, "ldc", qsb[:], qfull.rearrange("(a p) t -> p a t", p=128))
    c.dma(sy, "ldc", b4row[:], b4[:])
    ldc_total = c.cnt["ldc"]
    c.inc(v.memset(onesc[:], 1.0), "cv")
    c.inc(v.memset(onesr[:], 1.0), "cv")
    ones_tok = c.mile("cv")
    t.wait_ge(c._sems["ldc"], ldc_total)
    t.wait_ge(c._sems["cv"], ones_tok)
    for hh in range(2):
        mm = t.matmul(op[0][hh][:], onesr[:], b4row[:, hh * 512:(hh + 1) * 512],
                      start=True, stop=True)
        tk = c.inc(mm, "ct")
        c.wait(v, "ct", tk)
        c.inc(v.tensor_copy(b4rep[:, hh * 512:(hh + 1) * 512], op[0][hh][:]), "cv")
    b4_tok = c.mile("cv")

    kc = [c.sb(f"kc{j}", [128, DC, KCH], BF16) for j in range(2)]
    vc = [c.sb(f"vc{j}", [128, KTPC, D], BF16) for j in range(2)]
    pt = [c.sb(f"pt{j}", [128, 256], BF16) for j in range(2)]
    xr = c.sb("xr", [128, 2, D], F32)
    y2r = c.sb("y2r", [128, 2, D], BF16)
    mr = c.sb("mr", [128, 2, D], BF16)
    f0r = c.sb("f0r", [128, D], F32)
    zr = c.sb("zr", [128, D], F32)
    x1r = [c.sb(f"x1r{j}", [128, D], F32) for j in range(2)]
    rp = c.sb("rp", [128, 2], F32)

    kv_done = [0, 0]
    row_done = 0
    x1_st = [0, 0]
    pt_done = [0, 0]
    ss_done = [0, 0]

    for qp in range(8):
        q0 = qp * 256
        lxq = c.dma(sy, "lxq", xr[:],
                    x[q0:q0 + 256, :].rearrange("(a p) n -> p a n", p=128),
                    gates=[("cv", row_done)])
        ly2 = c.dma(sy, "ly2", y2r[:],
                    y2[q0:q0 + 256, :].rearrange("(a p) n -> p a n", p=128),
                    gates=[("cv", row_done)])
        lmr = c.dma(sy, "lmr", mr[:],
                    mtm[q0:q0 + 256, :].rearrange("(a p) n -> p a n", p=128),
                    gates=[("cv", row_done)])
        kt_global = 0
        t.wait_ge(c._sems["cv"], row_done)
        for kk in range(NKC):
            j = kk % 2
            lk = c.dma(g, f"lkc{j}", kc[j][:],
                       kfull_g.rearrange("(r a p) t -> r p a t", p=128, r=NCORE)[
                           kk // 2, :, :, (kk % 2) * KCH:(kk % 2 + 1) * KCH],
                       gates=[("ct", kv_done[j])])
            lv = c.dma(g, f"lvc{j}", vc[j][:],
                       vw4full[kk * KCH:(kk + 1) * KCH, :].rearrange("(a p) n -> p a n", p=128),
                       gates=[("ct", kv_done[j])])
            t.wait_ge(c._sems[f"lkc{j}"], lk)
            t.wait_ge(c._sems[f"lvc{j}"], lv)
            for ktl in range(KTPC):
                kt = kt_global
                sb_ = kt % 2
                t.wait_ge(c._sems["cs"], ss_done[sb_])
                for dc in range(DC):
                    mm = t.matmul(ss[sb_][:], kc[j][:, dc, ktl * 128:(ktl + 1) * 128],
                                  qsb[:, dc, q0:q0 + 256],
                                  start=(dc == 0), stop=(dc == DC - 1))
                qk_tok = c.inc(mm, "ct")
                s.wait_ge(c._sems["ct"], max(qk_tok, pt_done[sb_]))
                ex = c.inc(s.activation(out=pt[sb_][:], in_=ss[sb_][:], func=AF.Exp,
                                        scale=1.0 / 32), "cs")
                ss_done[sb_] = ex
                c.mile("cs")
                t.wait_ge(c._sems["cs"], ex)
                for qs in range(2):
                    psl = pt[sb_][:, qs * 128:(qs + 1) * 128]
                    mm = t.matmul(su[qs][:], psl, onesc[:],
                                  start=(kt == 0), stop=(kt == T // 128 - 1))
                    for hh in range(2):
                        mm = t.matmul(op[qs][hh][:], psl,
                                      vc[j][:, ktl, hh * 512:(hh + 1) * 512],
                                      start=(kt == 0), stop=(kt == T // 128 - 1))
                pt_done[sb_] = c.inc(mm, "ct")
                kt_global += 1
            kv_done[j] = c.cnt["ct"]
            c.mile("ct")
        pv_tok = c.cnt["ct"]
        sy_ = c._sems
        v.wait_ge(sy_["ct"], pv_tok)
        r1 = c.inc(v.reciprocal(rp[:, 0:1], su[0][:]), "cv")
        c.wait(v, "cv", r1)
        r2 = c.inc(v.reciprocal(rp[:, 1:2], su[1][:]), "cv")
        c.wait(v, "cv", r2)
        v.wait_ge(sy_["lxq"], lxq)
        v.wait_ge(sy_["ly2"], ly2)
        v.wait_ge(sy_["lmr"], lmr)
        v.wait_ge(sy_["cv"], b4_tok)
        for qs in range(2):
            sx1 = f"stx{qs}"
            if sx1 in c._sems:
                v.wait_ge(sy_[sx1], x1_st[qs])
            for hh in range(2):
                c.inc(v.scalar_tensor_tensor(
                    out=zr[:, hh * 512:(hh + 1) * 512], in0=op[qs][hh][:],
                    scalar=rp[:, qs:qs + 1], in1=b4rep[:, hh * 512:(hh + 1) * 512],
                    op0=ALU.mult, op1=ALU.add), "cv")
            zt = c.mile("cv")
            f1_ = c.inc(v.tensor_tensor(out=f0r[:], in0=y2r[:, qs, :], in1=mr[:, qs, :],
                                        op=ALU.mult), "cv")
            c.wait(v, "cv", f1_)
            a1_ = c.inc(v.tensor_tensor(out=x1r[qs][:], in0=zr[:], in1=xr[:, qs, :],
                                        op=ALU.add), "cv")
            c.wait(v, "cv", a1_)
            a2_ = c.inc(v.tensor_tensor(out=x1r[qs][:], in0=x1r[qs][:], in1=f0r[:],
                                        op=ALU.add), "cv")
            c.mile("cv")
            ti = qp * 2 + qs
            x1_st[qs] = c.dma(g, sx1, x1o[ti * 128:(ti + 1) * 128, :], x1r[qs][:],
                              gates=[("cv", a2_)])
        row_done = c.mile("cv")
    for nm in ["stx0", "stx1"]:
        g.wait_ge(c._sems[nm], c.cnt[nm])
    c.cur_es.close()
    c.cur_es = None


def phase_C(c, io):
    v, s, t, g, sy = c.v, c.s, c.t, c.g, c.sy
    c.prefix = "C_"
    c.cur_es = ExitStack()
    x1 = io["x1d"]
    norm_g = io["norm_g"]
    norm_b = io["norm_b"]
    wct = io["wct"]
    bcon = io["bcon"]
    wcgt = io["wcgt"]
    bcg = io["bcg"]
    wgt = io["wgt"]
    bgate = io["bgate"]
    wspt = io["wspt"]
    wfgt = io["wfgt"]
    bfg = io["bfg"]
    wtgt = io["wtgt"]
    btg = io["btg"]
    wbandt = io["wbandt"]
    bband = io["bband"]
    wgmoet = io["wgmoet"]
    lnsc = io["lnsc"]
    lnbi = io["lnbi"]
    w1e = io["w1e"]
    selrep = io["selrep"]
    b1e = io["b1e"]
    w2e = io["w2e"]
    b2e = io["b2e"]

    u_o = io["u_d"]
    aux_o = io["aux_d"]
    cp_o = io["cp_d"]
    gate_o = io["gate_d"]
    ymoe_o = io["ymoe_d"]

    pj = [c.ps(f"pj{j}", [128, 512], F32) for j in range(2)]
    pk = [c.ps(f"pk{j}", [128, 512], F32) for j in range(2)]

    bcon_c, _ = c.bias_col(bcon, D, "bcon_c", pj[0])
    bcg_c, _ = c.bias_col(bcg, D, "bcg_c", pj[0])
    bgate_c, _ = c.bias_col(bgate, D, "bgate_c", pj[0])
    bfg_c, _ = c.bias_col(bfg, NF, "bfg_c", pj[0])
    scols, bcols_e = [], []
    for e in range(8):
        sc_, _ = c.bias_col(lnsc[e, :], D, f"scol{e}", pj[0])
        bc_, _ = c.bias_col(lnbi[e, :], D, f"bcol{e}", pj[0])
        scols.append(sc_)
        bcols_e.append(bc_)
    cols_tok = c.mile("cv")

    btg_sb = c.sb("btg_sb", [1, 1], F32)
    bband_r = c.sb("bband_r", [1, 4], F32)
    bband_c = c.sb("bband_c", [4, 1], F32)
    b2e_sb = c.sb("b2e_sb", [8, D], BF16)
    sel_sb = c.sb("sel_sb", [8, 8 * 128], BF16)
    eps19 = c.sb("eps19", [4, 1], F32)
    onesr = c.sb("onesr", [1, 128], F32)
    onesrb = c.sb("onesrb", [1, 128], BF16)
    c.dma(sy, "ldc", btg_sb[:], btg[:])
    c.dma(sy, "ldc", bband_r[:], bband[:])
    c.dma(g, "ldc8", b2e_sb[:], b2e[:])
    c.dma(g, "ldc8", sel_sb[:], selrep[:])
    ldc_tok = c.cnt["ldc"]
    c.inc(v.memset(eps19[:], 1e-19), "cv")
    c.inc(v.memset(onesr[:], 1.0), "cv")
    c.inc(v.memset(onesrb[:], 1.0), "cv")
    ones_tok = c.mile("cv")
    t.wait_ge(c._sems["ldc"], ldc_tok)
    t.wait_ge(c._sems["cv"], c._bias_prev_ev)
    tb = c.inc(t.transpose(pj[0][0:4, 0:1], bband_r[0:1, 0:4],
                           c.ident_f32[0:1, 0:1]), "ct")
    c.wait(v, "ct", tb)
    c.inc(v.tensor_copy(bband_c[0:4, 0:1], pj[0][0:4, 0:1]), "cv")
    bband_tok = c.mile("cv")

    g_rep = c.sb("g_rep", [128, D], F32)
    b_rep = c.sb("b_rep", [128, D], F32)
    grow = c.sb("grow", [1, D], F32)
    brow = c.sb("brow", [1, D], F32)
    c.dma(sy, "ldc2", grow[:], norm_g.rearrange("(a b) -> a b", a=1))
    ldc2a = c.dma(sy, "ldc2", brow[:], norm_b.rearrange("(a b) -> a b", a=1))
    t.wait_ge(c._sems["ldc2"], ldc2a)
    t.wait_ge(c._sems["cv"], ones_tok)
    for which, (row, rep) in enumerate([(grow, g_rep), (brow, b_rep)]):
        for hh in range(2):
            if hh or which:
                t.wait_ge(c._sems["cv"], c.cnt["cv"])
            mm = t.matmul(pk[which][:], onesr[:], row[:, hh * 512:(hh + 1) * 512],
                          start=True, stop=True)
            tk = c.inc(mm, "ct")
            c.wait(v, "ct", tk)
            c.inc(v.tensor_copy(rep[:, hh * 512:(hh + 1) * 512], pk[which][:]), "cv")
    rep_tok = c.mile("cv")

    xhat_fm = c.sb("xhat_fm", [128, DC, TC], BF16)
    mask_fm = c.sb("mask_fm", [8, TC], BF16)
    es_mid = ExitStack()
    xni_fm = c.sb("xni_fm", [128, DC, TC], BF16, es_mid)

    es1 = ExitStack()
    tp = [c.ps(f"tp{j}", [128, 128], BF16, es1) for j in range(2)]
    tpq = [c.ps(f"tpq{j}", [128, 128], F32, es1) for j in range(2)]
    xn1_fm = c.sb("xn1_fm", [128, DC, TC], F32, es1)
    xt = [c.sb(f"xt{j}", [128, D], F32, es1) for j in range(2)]
    scr = c.sb("scr", [128, D], F32, es1)
    scr2 = c.sb("scr2", [128, D], F32, es1)
    xb = c.sb("xb", [128, D], BF16, es1)
    sx = c.sb("sx", [128, 1], F32, es1)
    ssq = c.sb("ssq", [128, 1], F32, es1)
    mu = c.sb("mu", [128, 1], F32, es1)
    musq = c.sb("musq", [128, 1], F32, es1)
    var = c.sb("var", [128, 1], F32, es1)
    rstd = c.sb("rstd", [128, 1], F32, es1)
    eps = c.sb("eps", [128, 1], F32, es1)
    junk = c.sb("junk", [128, D], BF16, es1)
    dump = c.sb("dump", [128, D], F32, es1)

    c.inc(v.memset(eps[:], 1e-5), "cv")
    c.mile("cv")

    def ln_stats(xi, ldsem=None, ldt=0, gate_cv=0):
        if ldsem:
            s.wait_ge(c._sems[ldsem], ldt)
        if gate_cv:
            s.wait_ge(c._sems["cv"], gate_cv)
        t1 = c.inc(s.activation(out=junk[:], in_=xi[:], func=AF.Square,
                                accum_out=ssq[:]), "cs")
        if ldsem:
            v.wait_ge(c._sems[ldsem], ldt)
        t2 = c.inc(v.tensor_scalar(out=dump[:], in0=xi[:], scalar1=0.0, scalar2=None,
                                   op0=ALU.add, op1=ALU.add, accum_out=sx[:]), "cv")
        c.wait(v, "cv", t2)
        t3 = c.inc(v.tensor_scalar(out=mu[:], in0=sx[:], scalar1=1.0 / D,
                                   scalar2=None, op0=ALU.mult), "cv")
        c.wait(v, "cv", t3)
        t4 = c.inc(v.tensor_tensor(out=musq[:], in0=mu[:], in1=mu[:], op=ALU.mult),
                   "cv")
        c.wait(v, "cs", t1)
        c.wait(v, "cv", t4)
        t5 = c.inc(v.scalar_tensor_tensor(out=var[:], in0=ssq[:], scalar=1.0 / D,
                                          in1=musq[:], op0=ALU.mult,
                                          op1=ALU.subtract), "cv")
        c.wait(s, "cv", t5)
        t6 = c.inc(s.activation(out=rstd[:], in_=var[:], func=AF.Ln, bias=eps[:]),
                   "cs")
        c.wait(s, "cs", t6)
        t7 = c.inc(s.activation(out=rstd[:], in_=rstd[:], func=AF.Exp, scale=-0.5),
                   "cs")
        c.wait(v, "cs", t7)

    tp_free = [0, 0]
    tpq_free = [0, 0]

    def transpose_tile(src_bf, dst_fm, i, src_tok):
        for cc in range(DC):
            j = cc % 2
            t.wait_ge(c._sems["cv"], max(tp_free[j], src_tok))
            tt = c.inc(t.transpose(tp[j][:], src_bf[:, cc * 128:(cc + 1) * 128],
                                   c.ident_bf[:]), "ct")
            c.wait(v, "ct", tt)
            ev = c.inc(v.tensor_copy(dst_fm[:, cc, i * 128:(i + 1) * 128], tp[j][:]),
                       "cv")
            tp_free[j] = ev
        return c.cnt["ct"]

    tile_done = [0] * NT
    for i in range(NT):
        lxs = f"lx{i % 2}"
        gates = [("ct", tile_done[i - 2])] if i >= 2 else []
        ldt = c.dma(sy, lxs, xt[i % 2][:], x1[i * 128:(i + 1) * 128, :], gates)
        xi = xt[i % 2]
        ln_stats(xi, lxs, ldt)
        t8 = c.inc(v.tensor_scalar(out=scr[:], in0=xi[:], scalar1=mu[:],
                                   scalar2=rstd[:], op0=ALU.subtract,
                                   op1=ALU.mult), "cv")
        tk1 = c.mile("cv")
        for cc in range(DC):
            j = cc % 2
            t.wait_ge(c._sems["cv"], max(tpq_free[j], tk1))
            tt = c.inc(t.transpose(tpq[j][:], scr[:, cc * 128:(cc + 1) * 128],
                                   c.ident_f32[:]), "ct")
            c.wait(v, "ct", tt)
            ev = c.inc(v.tensor_copy(xn1_fm[:, cc, i * 128:(i + 1) * 128],
                                     tpq[j][:]), "cv")
            tpq_free[j] = ev
        ct1 = c.cnt["ct"]
        v.wait_ge(c._sems["ct"], ct1)
        ln_stats(scr, gate_cv=tk1)
        t10 = c.inc(v.tensor_scalar(out=scr2[:], in0=scr[:], scalar1=mu[:],
                                    scalar2=rstd[:], op0=ALU.subtract,
                                    op1=ALU.mult), "cv")
        c.wait(v, "cv", t10)
        t11 = c.inc(v.tensor_copy(xb[:], scr2[:]), "cv")
        tk2 = c.mile("cv")
        ct2 = transpose_tile(xb, xhat_fm, i, tk2)
        v.wait_ge(c._sems["ct"], ct2)
        v.wait_ge(c._sems["cv"], rep_tok)
        t12 = c.inc(v.tensor_tensor(out=scr2[:], in0=scr2[:], in1=g_rep[:],
                                    op=ALU.mult), "cv")
        c.wait(v, "cv", t12)
        t13 = c.inc(v.tensor_tensor(out=scr2[:], in0=scr2[:], in1=b_rep[:],
                                    op=ALU.add), "cv")
        c.wait(v, "cv", t13)
        t14 = c.inc(v.tensor_copy(xb[:], scr2[:]), "cv")
        tk3 = c.mile("cv")
        ct3 = transpose_tile(xb, xni_fm, i, tk3)
        tile_done[i] = ct3
        if i < NT - 1:
            v.wait_ge(c._sems["ct"], ct3)
    fm_ready = c.mile("cv")

    wg_sb = c.sb("wg_sb", [128, DC, 8], F32, es1)
    lg_fm = c.sb("lg_fm", [8, TC], F32, es1)
    lgt = c.sb("lgt", [128, 8], F32, es1)
    lmax = c.sb("lmax", [128, 1], F32, es1)
    mraw = c.sb("mraw", [128, 8], F32, es1)
    mok = c.sb("mok", [128, 1], F32, es1)
    mtm_t = c.sb("mtm_t", [128, 8], BF16, es1)
    lwg = c.dma(g, "lwg", wg_sb[:], wgmoet.rearrange("(a p) n -> p a n", p=128))
    t.wait_ge(c._sems["lwg"], lwg)
    t.wait_ge(c._sems["cv"], fm_ready)
    lg_ev = 0
    for qb in range(TC // 512):
        bank = qb % 2
        if lg_ev:
            t.wait_ge(c._sems["cs"], lg_ev)
        for dc in range(DC):
            mm = t.matmul(pj[bank][0:8, :], wg_sb[:, dc, :],
                          xn1_fm[:, dc, qb * 512:(qb + 1) * 512],
                          start=(dc == 0), stop=(dc == DC - 1))
        mt = c.inc(mm, "ct")
        s.wait_ge(c._sems["ct"], mt)
        lg_ev = c.inc(s.activation(out=lg_fm[:, qb * 512:(qb + 1) * 512],
                                   in_=pj[bank][0:8, :], func=AF.Copy), "cs")
    lg_tok = c.mile("cs")
    mprev_ev = 0
    for i in range(NT):
        j = i % 2
        t.wait_ge(c._sems["cs"], lg_tok)
        t.wait_ge(c._sems["cv"], mprev_ev)
        ttk = c.inc(t.transpose(pj[0][0:128, 0:8],
                                lg_fm[0:8, i * 128:(i + 1) * 128],
                                c.ident_f32[0:8, 0:8]), "ct")
        c.wait(v, "ct", ttk)
        cv1 = c.inc(v.tensor_copy(lgt[:], pj[0][0:128, 0:8]), "cv")
        c.wait(v, "cv", cv1)
        cv2 = c.inc(v.tensor_reduce(out=lmax[:], in_=lgt[:],
                                    axis=mybir.AxisListType.X, op=ALU.max), "cv")
        c.wait(v, "cv", cv2)
        cv3 = c.inc(v.tensor_scalar(out=mraw[:], in0=lgt[:], scalar1=lmax[:],
                                    scalar2=None, op0=ALU.is_ge), "cv")
        c.wait(v, "cv", cv3)
        c.inc(v.memset(mok[:], 1.0), "cv")
        c.mile("cv")
        for e in range(8):
            cm1 = c.inc(v.tensor_tensor(out=mtm_t[:, e:e + 1], in0=mraw[:, e:e + 1],
                                        in1=mok[:], op=ALU.mult), "cv")
            c.wait(v, "cv", cm1)
            cm2 = c.inc(v.scalar_tensor_tensor(out=mok[:], in0=mtm_t[:, e:e + 1],
                                               scalar=-1.0, in1=mok[:],
                                               op0=ALU.mult, op1=ALU.add), "cv")
            c.wait(v, "cv", cm2)
        mk_tok = c.mile("cv")
        t.wait_ge(c._sems["cv"], mk_tok)
        ttk2 = c.inc(t.transpose(tp[j][0:8, :], mtm_t[:, 0:8], c.ident_bf[:]), "ct")
        c.wait(v, "ct", ttk2)
        mprev_ev = c.inc(v.tensor_copy(mask_fm[0:8, i * 128:(i + 1) * 128],
                                       tp[j][0:8, :]), "cv")
    mask_tok = c.mile("cv")
    es1.close()

    es2 = ExitStack()
    w_sb = [c.sb(f"w_sb{j}", [128, DC, D], BF16, es2) for j in range(2)]
    sigcg = c.sb("sigcg", [128, DC, TC], BF16, es2)
    fg_fm = c.sb("fg_fm", [128, 4, TC], BF16, es2)
    ost = [c.sb(f"ost{j}", [128, 512], BF16, es2) for j in range(2)]

    def load_w(dram_w, buf, stream, gate_ct=0, cols=D, extra_gates=()):
        gates = ([("ct", gate_ct)] if gate_ct else []) + list(extra_gates)
        return c.dma(g, stream, buf[:, :, 0:cols],
                     dram_w.rearrange("(a p) n -> p a n", p=128), gates)

    ost_free = [0, 0]
    ost_cv = [0, 0]

    def proj2(wbuf, wsem, wtok_, bias, nout, act, dst_fm=None, dst_dram=None,
              mul_fm=None, mul_cols=None):
        t.wait_ge(c._sems[wsem], wtok_)
        t.wait_ge(c._sems["cv"], fm_ready)
        oc_n = nout // 128
        pj_free = [c.cnt["cs"]] * 2
        for oc in range(oc_n):
            for qb in range(TC // 512):
                bank = (oc * (TC // 512) + qb) % 2
                t.wait_ge(c._sems["cs"], pj_free[bank])
                if mul_fm is not None:
                    t.wait_ge(c._sems["cv"], ost_cv[bank])
                for dc in range(DC):
                    mm = t.matmul(pj[bank][:], wbuf[:, dc, oc * 128:(oc + 1) * 128],
                                  xni_fm[:, dc, qb * 512:(qb + 1) * 512],
                                  start=(dc == 0), stop=(dc == DC - 1))
                mt = c.inc(mm, "ct")
                s.wait_ge(c._sems["ct"], mt)
                kwargs = {}
                if bias is not None:
                    kwargs["bias"] = bias[:, oc:oc + 1]
                if dst_fm is not None:
                    ep = c.inc(s.activation(out=dst_fm[:, oc, qb * 512:(qb + 1) * 512],
                                            in_=pj[bank][:], func=act, **kwargs),
                               "cs")
                    pj_free[bank] = ep
                else:
                    stn = f"os{bank}"
                    if stn in c._sems:
                        s.wait_ge(c._sems[stn], ost_free[bank])
                    ep = c.inc(s.activation(out=ost[bank][:], in_=pj[bank][:],
                                            func=act, **kwargs), "cs")
                    pj_free[bank] = ep
                    if mul_fm is not None:
                        mc = oc % mul_cols
                        v.wait_ge(c._sems["cs"], ep)
                        if stn in c._sems:
                            v.wait_ge(c._sems[stn], ost_free[bank])
                        c.inc(v.tensor_tensor(
                            out=ost[bank][:], in0=ost[bank][:],
                            in1=mul_fm[:, mc, qb * 512:(qb + 1) * 512],
                            op=ALU.mult), "cv")
                        ost_cv[bank] = c.mile("cv")
                        gate_sem = ("cv", ost_cv[bank])
                    else:
                        gate_sem = ("cs", ep)
                    ost_free[bank] = c.dma(
                        g, stn,
                        dst_dram[oc * 128:(oc + 1) * 128, qb * 512:(qb + 1) * 512],
                        ost[bank][:], gates=[gate_sem])
        return c.cnt["cs"], c.cnt["ct"]

    ph1_gates = (("cv", mask_tok), ("ct", c.cnt["ct"]), ("cs", c.cnt["cs"]))
    wt1 = load_w(wcgt, w_sb[0], "wd0", extra_gates=ph1_gates)
    _, ct_a = proj2(w_sb[0], "wd0", wt1, bcg_c, D, AF.Sigmoid, dst_fm=sigcg)
    wt2 = load_w(wfgt, w_sb[1], "wd1", cols=NF, extra_gates=ph1_gates)
    _, ct_b = proj2(w_sb[1], "wd1", wt2, bfg_c, NF, AF.Sigmoid, dst_fm=fg_fm)
    wt3 = load_w(wct, w_sb[0], "wd0", gate_ct=ct_a)
    _, ct_c = proj2(w_sb[0], "wd0", wt3, bcon_c, D, AF.Identity, dst_dram=cp_o,
                    mul_fm=sigcg, mul_cols=DC)
    wt4 = load_w(wgt, w_sb[1], "wd1", gate_ct=ct_b)
    _, ct_d = proj2(w_sb[1], "wd1", wt4, bgate_c, D, AF.Sigmoid, dst_dram=gate_o)
    wt5 = load_w(wspt, w_sb[0], "wd0", gate_ct=ct_c)
    _, ct_e = proj2(w_sb[0], "wd0", wt5, None, D, AF.Copy, dst_dram=u_o,
                    mul_fm=fg_fm, mul_cols=4)

    wband_sb = c.sb("wband_sb", [128, DC, 4], BF16, es2)
    l2 = c.dma(g, "lband", wband_sb[:], wbandt.rearrange("(a p) n -> p a n", p=128))
    wtgf, wtg_tok = c.bias_col(wtgt.rearrange("a b -> (a b)"), D, "wtgf", pj[0], es=es2,
                               t_gates=(("cs", c.cnt["cs"]),),
                               dma_gates=(("cs", c.cnt["cs"]), ("cv", c.cnt["cv"]),
                                          ("ct", c.cnt["ct"])))
    wtg_sb2 = c.sb("wtg_sb2", [128, DC], BF16, es2)
    c.wait(v, "cv", wtg_tok)
    c.inc(v.tensor_copy(wtg_sb2[:], wtgf[:]), "cv")
    wtg_tok2 = c.mile("cv")
    ts_sb = c.sb("ts_sb", [1, TC], F32, es2)
    bs_sb = c.sb("bs_sb", [4, TC], F32, es2)
    t.wait_ge(c._sems["cv"], wtg_tok2)
    t.wait_ge(c._sems["lband"], l2)
    t.wait_ge(c._sems["ct"], ct_e)
    prev = 0
    for qb in range(TC // 512):
        bank = qb % 2
        if prev:
            t.wait_ge(c._sems["cs"], prev)
        for dc in range(DC):
            mm = t.matmul(pk[bank][0:1, :], wtg_sb2[:, dc:dc + 1],
                          xni_fm[:, dc, qb * 512:(qb + 1) * 512],
                          start=(dc == 0), stop=(dc == DC - 1))
        m1 = c.inc(mm, "ct")
        for dc in range(DC):
            mm = t.matmul(pj[bank][0:4, :], wband_sb[:, dc, :],
                          xni_fm[:, dc, qb * 512:(qb + 1) * 512],
                          start=(dc == 0), stop=(dc == DC - 1))
        m2 = c.inc(mm, "ct")
        s.wait_ge(c._sems["ct"], m2)
        c.inc(s.activation(out=ts_sb[:, qb * 512:(qb + 1) * 512],
                           in_=pk[bank][0:1, :], func=AF.Sigmoid,
                           bias=btg_sb[:]), "cs")
        s.wait_ge(c._sems["cv"], bband_tok)
        e2 = c.inc(s.activation(out=bs_sb[:, qb * 512:(qb + 1) * 512],
                                in_=pj[bank][0:4, :], func=AF.Sigmoid,
                                bias=bband_c[0:4, 0:1]), "cs")
        c.wait(s, "cs", e2)
        c.inc(s.activation(out=bs_sb[:, qb * 512:(qb + 1) * 512],
                           in_=bs_sb[:, qb * 512:(qb + 1) * 512], func=AF.Ln,
                           bias=eps19[:]), "cs")
        prev = c.mile("cs")
    aux_tok = c.mile("cs")
    c.dma(g, "sta", aux_o[0:1, :], ts_sb[:], gates=[("cs", aux_tok)])
    c.dma(g, "sta", aux_o[1:5, :], bs_sb[:])
    for nm, val in [("os0", ost_free[0]), ("os1", ost_free[1]),
                    ("sta", c.cnt["sta"])]:
        if nm in c._sems:
            g.wait_ge(c._sems[nm], val)
    g.wait_ge(c._sems["ct"], c.cnt["ct"])
    g.wait_ge(c._sems["cs"], c.cnt["cs"])
    io["_ag"](io["u_d"], io["u_g"], "cc2")
    io["_ag"](io["aux_d"], io["aux_g"], "cc3")
    ph2_done = tuple([("cs", c.cnt["cs"]), ("cv", c.cnt["cv"]), ("ct", c.cnt["ct"])]
                     + [(nm, c.cnt[nm]) for nm in ("sta", "os0", "os1")
                        if nm in c._sems])
    es2.close()
    es_mid.close()

    h_all = c.sb("h_all", [128, 8, 2, 512], BF16)
    w2e_sb = c.sb("w2e_sb", [128, 8, 2, D], BF16)
    w1p_all = c.sb("w1p_all", [128, 8, DC, 256], BF16)
    b1tc = c.sb("b1tc", [128, 8, 2], F32)
    mrep = c.sb("mrep", [128, 512], F32)
    ymfm = c.sb("ymfm", [128, DC, 512], F32)
    ymtm = [c.sb(f"ymtm{j}", [128, D], F32) for j in range(2)]
    es_fold = ExitStack()
    w1e_f = [c.sb(f"w1e_f{j}", [128, DC, 256], F32, es_fold) for j in range(2)]
    b1er = c.sb("b1er", [2, 128], F32, es_fold)
    b1es = c.sb("b1es", [128, 2], F32, es_fold)
    tpf = [c.ps(f"tpf{j}", [128, 128], F32) for j in range(2)]

    lw2 = c.dma(g, "lw2e", w2e_sb[:], w2e.rearrange("e (a p) n -> p e a n", p=128))

    w1tok = [0, 0]
    w1tok[0] = c.dma(g, "lw1e0", w1e_f[0][:],
                     w1e[0].rearrange("(a p) n -> p a n", p=128))
    w1tok[1] = c.dma(g, "lw1e1", w1e_f[1][:],
                     w1e[1].rearrange("(a p) n -> p a n", p=128))
    b1_prev_ct = 0
    lb_tok = 0
    for e in range(8):
        par = e % 2
        v.wait_ge(c._sems[f"lw1e{par}"], w1tok[par])
        v.wait_ge(c._sems["cv"], cols_tok)
        for dc in range(DC):
            c.inc(v.tensor_scalar(out=w1p_all[:, e, dc, :],
                                  in0=w1e_f[par][:, dc, :],
                                  scalar1=scols[e][:, dc:dc + 1],
                                  scalar2=None, op0=ALU.mult), "cv")
        wf_tok = c.mile("cv")
        t.wait_ge(c._sems["cv"], wf_tok)
        if b1_prev_ct:
            t.wait_ge(c._sems["cv"], b1_cv)
        for hh in range(2):
            for dc in range(DC):
                mm = t.matmul(tpf[hh][0:128, 0:1],
                              w1e_f[par][:, dc, hh * 128:(hh + 1) * 128],
                              bcols_e[e][:, dc:dc + 1],
                              start=(dc == 0), stop=(dc == DC - 1))
            c.inc(mm, "ct")
        bt_ct = c.cnt["ct"]
        lb_gates = list(ph2_done) if e == 0 else [("cv", b1_cv)]
        lb_tok = c.dma(sy, "lb1e", b1er[:],
                       b1e[e].rearrange("(a b) -> a b", b=128), gates=lb_gates)
        v.wait_ge(c._sems["ct"], bt_ct)
        v.wait_ge(c._sems["lb1e"], lb_tok)
        t.wait_ge(c._sems["lb1e"], lb_tok)
        ttb = c.inc(t.transpose(pk[0][0:128, 0:2], b1er[0:2, :],
                                c.ident_f32[0:2, 0:2]), "ct")
        c.wait(v, "ct", ttb)
        cb_ = c.inc(v.tensor_copy(b1es[:], pk[0][0:128, 0:2]), "cv")
        c.wait(v, "cv", cb_)
        for hh in range(2):
            c.inc(v.tensor_tensor(out=b1tc[:, e, hh:hh + 1],
                                  in0=tpf[hh][0:128, 0:1],
                                  in1=b1es[:, hh:hh + 1], op=ALU.add), "cv")
        b1_cv = c.mile("cv")
        b1_prev_ct = bt_ct
        if e < 6:
            w1tok[par] = c.dma(g, f"lw1e{par}", w1e_f[par][:],
                               w1e[e + 2].rearrange("(a p) n -> p a n", p=128),
                               gates=[("cv", wf_tok), ("ct", bt_ct)])
    fold_tok = c.mile("cv")
    g.wait_ge(c._sems["cv"], fold_tok)
    g.wait_ge(c._sems["ct"], c.cnt["ct"])
    es_fold.close()
    t.wait_ge(c._sems["lw2e"], lw2)
    t.wait_ge(c._sems["ldc8"], c.cnt["ldc8"])

    prev_yqb_ct = 0
    tpf_free = [0, 0]
    ymtm_st = [0, 0]
    mr_cv = 0
    for qb in range(TC // 512):
        hp_free = [c.cnt["cs"]] * 2
        hm_cv = 0
        for e in range(8):
            t.wait_ge(c._sems["cv"], max(mask_tok, mr_cv))
            if qb == 0 and e == 0:
                t.wait_ge(c._sems["ldc8"], c.cnt["ldc8"])
            mm = t.matmul(pk[1][:], sel_sb[:, e * 128:(e + 1) * 128],
                          mask_fm[0:8, qb * 512:(qb + 1) * 512],
                          start=True, stop=True)
            mr_ct = c.inc(mm, "ct")
            c.wait(v, "ct", mr_ct)
            mr_cv = c.inc(v.tensor_copy(mrep[:], pk[1][:]), "cv")
            c.mile("cv")
            t.wait_ge(c._sems["cv"], fold_tok)
            for hh in range(2):
                bank = hh
                t.wait_ge(c._sems["cs"], hp_free[bank])
                for dc in range(DC):
                    mm = t.matmul(pj[bank][:],
                                  w1p_all[:, e, dc, hh * 128:(hh + 1) * 128],
                                  xhat_fm[:, dc, qb * 512:(qb + 1) * 512],
                                  start=(dc == 0), stop=(dc == DC - 1))
                mt = c.inc(mm, "ct")
                s.wait_ge(c._sems["ct"], mt)
                if hm_cv:
                    s.wait_ge(c._sems["cv"], hm_cv)
                ep = c.inc(s.activation(out=h_all[:, e, hh, :], in_=pj[bank][:],
                                        func=AF.Gelu,
                                        bias=b1tc[:, e, hh:hh + 1]), "cs")
                hp_free[bank] = ep
                v.wait_ge(c._sems["cs"], ep)
                v.wait_ge(c._sems["cv"], mr_cv)
                hm = c.inc(v.tensor_tensor(out=h_all[:, e, hh, :],
                                           in0=h_all[:, e, hh, :],
                                           in1=mrep[:], op=ALU.mult), "cv")
            hm_cv = c.mile("cv")
        hmask_tok = c.mile("cv")
        t.wait_ge(c._sems["cv"], hmask_tok)
        yp_free = [c.cnt["cs"]] * 2
        for oc in range(DC):
            bank = oc % 2
            t.wait_ge(c._sems["cs"], yp_free[bank])
            first = True
            for e in range(8):
                for hh in range(2):
                    mm = t.matmul(pj[bank][:],
                                  w2e_sb[:, e, hh, oc * 128:(oc + 1) * 128],
                                  h_all[:, e, hh, :], start=first, stop=False)
                    first = False
            mm = t.matmul(pj[bank][:], b2e_sb[:, oc * 128:(oc + 1) * 128],
                          mask_fm[:, qb * 512:(qb + 1) * 512],
                          start=False, stop=True)
            mt = c.inc(mm, "ct")
            s.wait_ge(c._sems["ct"], mt)
            if prev_yqb_ct:
                s.wait_ge(c._sems["ct"], prev_yqb_ct)
            ep = c.inc(s.activation(out=ymfm[:, oc, :], in_=pj[bank][:],
                                    func=AF.Copy), "cs")
            yp_free[bank] = ep
        ym_tok = c.mile("cs")
        for tsub in range(4):
            jb = tsub % 2
            styn = f"sty{jb}"
            for oc in range(DC):
                j = oc % 2
                t.wait_ge(c._sems["cv"], tpf_free[j])
                t.wait_ge(c._sems["cs"], ym_tok)
                tt = c.inc(t.transpose(tpf[j][:],
                                       ymfm[:, oc, tsub * 128:(tsub + 1) * 128],
                                       c.ident_f32[:]), "ct")
                c.wait(v, "ct", tt)
                if oc == 0 and styn in c._sems:
                    v.wait_ge(c._sems[styn], ymtm_st[jb])
                ev = c.inc(v.tensor_copy(ymtm[jb][:, oc * 128:(oc + 1) * 128],
                                         tpf[j][:]), "cv")
                tpf_free[j] = ev
            ti = qb * 4 + tsub
            ymtm_st[jb] = c.dma(g, styn, ymoe_o[ti * 128:(ti + 1) * 128, :],
                                ymtm[jb][:], gates=[("cv", c.cnt["cv"])])
        prev_yqb_ct = c.cnt["ct"]
    for nm in ["sty0", "sty1"]:
        if nm in c._sems:
            g.wait_ge(c._sems[nm], c.cnt[nm])
    c.cur_es.close()
    c.cur_es = None


def fft_consts():
    N = 2 * T
    n2 = np.arange(128)[:, None]; k2 = np.arange(256)[None, :]
    W2 = np.exp(-2j * np.pi * n2 * k2 / 256.0)
    n1 = np.arange(128)[:, None]; k2f = np.arange(256)[None, :]
    TF = np.exp(-2j * np.pi * n1 * k2f / N)
    k1c = np.arange(128)[:, None]
    W1 = np.exp(-2j * np.pi * n1 * k1c.T / 128.0)
    V128 = np.exp(+2j * np.pi * k1c * np.arange(128)[None, :] / 128.0)
    t1 = np.arange(128)[None, :]
    out = {}
    bf = lambda x: np.ascontiguousarray(x).astype(ml_dtypes.bfloat16)
    f3 = lambda x: np.ascontiguousarray(x).astype(np.float32)
    out["W2r"] = bf(W2.real); out["W2i"] = bf(W2.imag); out["W2in"] = bf(-W2.imag)
    out["Tfr"] = f3(TF.real); out["Tfi"] = f3(TF.imag)
    out["W1r"] = bf(W1.real); out["W1i"] = bf(W1.imag); out["W1in"] = bf(-W1.imag)
    out["V128r"] = bf(V128.real); out["V128i"] = bf(V128.imag)
    out["V128in"] = bf(-V128.imag)
    for ch in range(2):
        k2v = (np.arange(128) + 128 * ch)[:, None]
        T2 = np.exp(+2j * np.pi * t1 * k2v / N)
        out[f"T2r{ch}"] = f3(T2.real); out[f"T2i{ch}"] = f3(T2.imag)
        t2v = np.arange(128)[None, :]
        W256 = np.exp(+2j * np.pi * k2v * t2v / 256.0) / N
        out[f"Q256r{ch}"] = bf(W256.real); out[f"Q256i{ch}"] = bf(W256.imag)
        out[f"Q256in{ch}"] = bf(-W256.imag)
    j_ = np.arange(128)
    tri = (j_[:, None] < j_[None, :]).astype(np.float32)
    out["TriS"] = tri
    tm = (128 * np.arange(128)[:, None] + np.arange(128)[None, :]).astype(np.float32)
    out["Tm0"] = tm
    out["Tm1"] = tm + 1.0
    cta = np.concatenate([out["Tfr"], out["Tfi"]], axis=1)
    ctb = np.concatenate([out["Tfi"], out["Tfr"]], axis=1)
    out["CTA"] = f3(np.tile(cta, (1, 4)))
    out["CTB"] = f3(np.tile(ctb, (1, 4)))
    c2a = np.concatenate([out[f"T2r0"], out[f"T2i0"], out[f"T2r1"], out[f"T2i1"]],
                         axis=1)
    c2b = np.concatenate([out[f"T2i0"], out[f"T2r0"], out[f"T2i1"], out[f"T2r1"]],
                         axis=1)
    out["CT2A"] = f3(np.tile(c2a, (1, 4)))
    out["CT2B"] = f3(np.tile(c2b, (1, 4)))
    return out


FFT_CONST_SPECS = {
    "W2r": ([128, 256], BF16), "W2i": ([128, 256], BF16),
    "W2in": ([128, 256], BF16), "Tfr": ([128, 256], F32),
    "Tfi": ([128, 256], F32), "W1r": ([128, 128], BF16),
    "W1i": ([128, 128], BF16), "W1in": ([128, 128], BF16),
    "V128r": ([128, 128], BF16), "V128i": ([128, 128], BF16),
    "V128in": ([128, 128], BF16),
    "T2r0": ([128, 128], F32), "T2i0": ([128, 128], F32),
    "T2r1": ([128, 128], F32), "T2i1": ([128, 128], F32),
    "Q256r0": ([128, 128], BF16), "Q256i0": ([128, 128], BF16),
    "Q256in0": ([128, 128], BF16),
    "Q256r1": ([128, 128], BF16), "Q256i1": ([128, 128], BF16),
    "Q256in1": ([128, 128], BF16),
    "TriS": ([128, 128], F32), "Tm0": ([128, 128], F32),
    "Tm1": ([128, 128], F32),
    "CTA": ([128, 2048], F32), "CTB": ([128, 2048], F32),
    "CT2A": ([128, 2048], F32), "CT2B": ([128, 2048], F32),
}


def phase_D(c, io):
    from concourse.bass_core_rust import IndirectOffsetOnAxis
    v, s, t, g, sy = c.v, c.s, c.t, c.g, c.sy
    c.prefix = "D_"
    c.cur_es = ExitStack()
    useq = io["useq_d"]
    u_g = io["u_g"]
    aux_g = io["aux_g"]
    uidx = io["uidx"]
    bsel = io["bsel"]
    ld_c = io["ld_c"]
    fi_c = io["fi_c"]
    fi_full = io["fi_full"]
    fr_c = io["fr_c"]
    dt_c = io["dt_c"]
    cn = {nm: io[nm] for nm in FFT_CONST_SPECS}
    spec_o = io["spec_d"]

    pb = [c.ps(f"pb{j}", [128, 512], F32) for j in range(8)]

    uidx_sb = c.sb("uidx_sb", [128, 8], mybir.dt.int32)
    usel = [c.sb(f"usel{j}", [128, TC], BF16) for j in range(2)]
    c.dma(sy, "uixl", uidx_sb[:], uidx[:])
    g.wait_ge(c._sems["uixl"], c.cnt["uixl"])
    if "ust" not in c._sems:
        c.sem("ust")
    if "uix" not in c._sems:
        c.sem("uix")
    for r in range(NCORE):
        jb = r % 2
        if r >= 2:
            g.wait_ge(c._sems["ust"], (r - 1) * 16)
        g.indirect_dma_start(usel[jb][:], None, u_g[:],
                             IndirectOffsetOnAxis(uidx_sb[:, r:r + 1], 0)
                             ).then_inc(c._sems["uix"], 16)
        c.cnt["uix"] += 16
        g.wait_ge(c._sems["uix"], c.cnt["uix"])
        g.dma_start(out=useq[:, r * TC:(r + 1) * TC], in_=usel[jb][:]).then_inc(
            c._sems["ust"], 16)
        c.cnt["ust"] += 16
    ust_total = c.cnt["ust"]

    cs_ = {}
    for nm in cn:
        cs_[nm] = c.sb("c_" + nm, list(cn[nm].tensor.shape), cn[nm].tensor.dtype)
        c.dma(sy, "ldc", cs_[nm][:], cn[nm][:])
    rows = {}
    for nm, ap_, w in [("ld", ld_c, CPC), ("fic", fi_c, CPC), ("fif", fi_full, NF),
                       ("fr", fr_c, CPC), ("dt", dt_c, CPC)]:
        rows[nm] = c.sb("r_" + nm, [1, w], F32)
        c.dma(sy, "ldc", rows[nm][:], ap_[:])
    TS = c.sb("TS", [128, 128], F32)
    L1n_all = [c.sb(f"L1n{b}", [128, 128], F32) for b in range(4)]
    aux_v = aux_g.rearrange("(r f) (i2 j) -> f r i2 j", f=5, j=128)
    c.dma(sy, "ldc", TS[:], aux_v[0])
    for b in range(4):
        c.dma(sy, "ldc", L1n_all[b][:], aux_v[1 + b])
    ldc_tok = c.cnt["ldc"]

    onesr = c.sb("onesr", [1, 128], F32)
    onescol = c.sb("onescol", [128, 1], F32)
    c.inc(v.memset(onesr[:], 1.0), "cv")
    c.inc(v.memset(onescol[:], 1.0), "cv")
    ones_tok = c.mile("cv")

    fmax = c.sb("fmax", [1, 1], F32)
    fneg = c.sb("fneg", [1, 1], F32)
    fsum = c.sb("fsum", [1, 1], F32)
    k0 = c.sb("k0", [1, 1], F32)
    efr = c.sb("efr", [1, NF], F32)
    v.wait_ge(c._sems["ldc"], ldc_tok)
    c1 = c.inc(v.tensor_reduce(out=fmax[:], in_=rows["fif"][:],
                               axis=mybir.AxisListType.X, op=ALU.max), "cv")
    c.wait(v, "cv", c1)
    c2 = c.inc(v.tensor_scalar(out=fneg[:], in0=fmax[:], scalar1=-1.0, scalar2=None,
                               op0=ALU.mult), "cv")
    c.wait(s, "cv", c2)
    c3 = c.inc(s.activation(out=efr[:], in_=rows["fif"][:], func=AF.Exp,
                            bias=fneg[:]), "cs")
    c.wait(v, "cs", c3)
    c4 = c.inc(v.tensor_reduce(out=fsum[:], in_=efr[:], axis=mybir.AxisListType.X,
                               op=ALU.add), "cv")
    c.wait(s, "cv", c4)
    c5 = c.inc(s.activation(out=fsum[:], in_=fsum[:], func=AF.Ln), "cs")
    c.wait(v, "cs", c5)
    c6 = c.inc(v.scalar_tensor_tensor(out=k0[:], in0=fmax[:], scalar=-1.0,
                                      in1=fsum[:], op0=ALU.mult,
                                      op1=ALU.subtract), "cv")
    c.wait(v, "cv", c6)
    c7 = c.inc(v.tensor_scalar(out=k0[:], in0=k0[:], scalar1=float(np.log(NF)),
                               scalar2=None, op0=ALU.add), "cv")
    k0_tok = c.mile("cv")

    reps = {}
    t.wait_ge(c._sems["ldc"], ldc_tok)
    t.wait_ge(c._sems["cv"], max(ones_tok, k0_tok))
    rep_ev = 0
    for nm in ["ld", "fic", "fr", "dt"]:
        reps[nm] = c.sb("rep_" + nm, [128, CPC], F32)
        if rep_ev:
            t.wait_ge(c._sems["cv"], rep_ev)
        mm = t.matmul(pb[0][:, 0:CPC], onesr[:], rows[nm][:], start=True, stop=True)
        tk = c.inc(mm, "ct")
        c.wait(v, "ct", tk)
        rep_ev = c.inc(v.tensor_copy(reps[nm][:], pb[0][:, 0:CPC]), "cv")
    k0rep = c.sb("k0rep", [128, 1], F32)
    t.wait_ge(c._sems["cv"], rep_ev)
    mm = t.matmul(pb[0][:, 0:1], onesr[:], k0[:], start=True, stop=True)
    tk = c.inc(mm, "ct")
    c.wait(v, "ct", tk)
    c.inc(v.tensor_copy(k0rep[:], pb[0][:, 0:1]), "cv")
    rep_tok = c.mile("cv")

    logdec = c.sb("logdec", [128, CPC], F32)
    omega = c.sb("omega", [128, CPC], F32)
    eps19c = c.sb("eps19c", [128, 1], F32)
    c.inc(v.memset(eps19c[:], 1e-19), "cv")
    ep_tok = c.mile("cv")
    s.wait_ge(c._sems["cv"], max(rep_tok, ep_tok))
    d1 = c.inc(s.activation(out=logdec[:], in_=reps["ld"][:], func=AF.Sigmoid), "cs")
    c.wait(s, "cs", d1)
    d2 = c.inc(s.activation(out=logdec[:], in_=logdec[:], func=AF.Ln,
                            bias=eps19c[:]), "cs")
    c.wait(v, "cs", d2)
    d3 = c.inc(v.tensor_tensor(out=logdec[:], in0=logdec[:], in1=reps["fic"][:],
                               op=ALU.add), "cv")
    c.wait(v, "cv", d3)
    d4 = c.inc(v.tensor_scalar(out=logdec[:], in0=logdec[:], scalar1=k0rep[:],
                               scalar2=None, op0=ALU.add), "cv")
    s.wait_ge(c._sems["cv"], d4)
    d5 = c.inc(s.activation(out=omega[:], in_=reps["dt"][:], func=AF.Sigmoid), "cs")
    c.wait(v, "cs", d5)
    d6 = c.inc(v.tensor_tensor(out=omega[:], in0=omega[:], in1=reps["fr"][:],
                               op=ALU.mult), "cv")
    c.wait(v, "cv", d6)
    d7 = c.inc(v.tensor_scalar(out=omega[:], in0=omega[:], scalar1=0.2, scalar2=None,
                               op0=ALU.mult), "cv")
    par_tok = c.mile("cv")

    L2 = c.sb("L2", [128, 128], F32)
    C1s = c.sb("C1s", [128, 128], F32)
    E_T = c.sb("E_T", [128, 128], F32)
    scol = c.sb("scol", [128, 1], F32)
    srow = c.sb("srow", [1, 128], F32)
    prow = c.sb("prow", [1, 128], F32)
    pcol = c.sb("pcol", [128, 1], F32)
    E_Tb = c.sb("E_Tb", [128, 128], F32)
    bsel_sb = c.sb("bsel_sb", [128, 4], F32)
    c.dma(sy, "lbse", bsel_sb[:], bsel[:])
    t.wait_ge(c._sems["ldc"], ldc_tok)
    t.wait_ge(c._sems["ld"], c.ld_idents)
    t.wait_ge(c._sems["cv"], rep_tok)
    for b in range(4):
        t.wait_ge(c._sems["cv"], c.cnt["cv"])
        tt = c.inc(t.transpose(pb[1][:, 0:128], L1n_all[b][:], c.ident_f32[:]), "ct")
        c.wait(v, "ct", tt)
        e1 = c.inc(v.tensor_copy(L2[:], pb[1][:, 0:128]), "cv")
        c.mile("cv")
        t.wait_ge(c._sems["cv"], e1)
        mm = t.matmul(pb[2][:, 0:128], cs_["TriS"][:], L2[:], start=True, stop=True)
        m1 = c.inc(mm, "ct")
        mm = t.matmul(pb[3][0:1, 0:128], onescol[:], L2[:], start=True, stop=True)
        m2 = c.inc(mm, "ct")
        c.wait(v, "ct", m1)
        e2 = c.inc(v.tensor_copy(C1s[:], pb[2][:, 0:128]), "cv")
        c.wait(v, "ct", m2)
        e3 = c.inc(v.tensor_copy(srow[:], pb[3][0:1, 0:128]), "cv")
        c.mile("cv")
        t.wait_ge(c._sems["cv"], e3)
        tt = c.inc(t.transpose(pb[4][:, 0:1], srow[:], c.ident_f32[0:1, 0:1]), "ct")
        c.wait(v, "ct", tt)
        e4 = c.inc(v.tensor_copy(scol[:], pb[4][:, 0:1]), "cv")
        c.mile("cv")
        t.wait_ge(c._sems["cv"], e4)
        mm = t.matmul(pb[5][:, 0:1], cs_["TriS"][:], scol[:], start=True, stop=True)
        m3 = c.inc(mm, "ct")
        c.wait(v, "ct", m3)
        e5b = c.inc(v.tensor_copy(pcol[:], pb[5][:, 0:1]), "cv")
        c.mile("cv")
        t.wait_ge(c._sems["cv"], e5b)
        tt = c.inc(t.transpose(pb[6][0:1, 0:128], pcol[:], c.ident_f32[:]), "ct")
        c.wait(v, "ct", tt)
        e6 = c.inc(v.tensor_copy(prow[:], pb[6][0:1, 0:128]), "cv")
        c.mile("cv")
        t.wait_ge(c._sems["cv"], e6)
        mm = t.matmul(pb[7][:, 0:128], onesr[:], prow[:], start=True, stop=True)
        m4 = c.inc(mm, "ct")
        c.wait(v, "ct", m4)
        e7 = c.inc(v.tensor_tensor(out=C1s[:], in0=C1s[:], in1=pb[7][:, 0:128],
                                   op=ALU.add), "cv")
        c.mile("cv")
        t.wait_ge(c._sems["cv"], e7)
        tt = c.inc(t.transpose(pb[1][:, 0:128], C1s[:], c.ident_f32[:]), "ct")
        c.wait(v, "ct", tt)
        e8 = c.inc(v.tensor_copy(E_Tb[:], pb[1][:, 0:128]), "cv")
        c.wait(v, "cv", e8)
        if b == 0:
            v.wait_ge(c._sems["lbse"], c.cnt["lbse"])
            e9 = c.inc(v.tensor_scalar(out=E_T[:], in0=E_Tb[:],
                                       scalar1=bsel_sb[:, 0:1], scalar2=None,
                                       op0=ALU.mult), "cv")
        else:
            e9 = c.inc(v.scalar_tensor_tensor(out=E_T[:], in0=E_Tb[:],
                                              scalar=bsel_sb[:, b:b + 1],
                                              in1=E_T[:], op0=ALU.mult,
                                              op1=ALU.add), "cv")
        c.wait(v, "cv", e9)
    et_tok = c.mile("cv")

    KR8b = [c.sb(f"KR8{j}", [128, 1024], BF16) for j in range(2)]
    KI8b = [c.sb(f"KI8{j}", [128, 1024], BF16) for j in range(2)]
    UR = [c.sb(f"UR{j}", [128, 8, 128], BF16) for j in range(2)]
    UI = [c.sb(f"UI{j}", [128, 8, 128], BF16) for j in range(2)]
    X8 = c.sb("X8", [128, 1024], F32)
    MM8 = c.sb("MM8", [128, 1024], F32)
    PH8 = c.sb("PH8", [128, 1024], F32)
    KK8 = c.sb("KK8", [128, 1024], F32)
    TH8 = c.sb("TH8", [128, 1024], F32)
    SIN8 = c.sb("SIN8", [128, 1024], F32)
    COS8 = c.sb("COS8", [128, 1024], F32)
    TS8 = c.sb("TS8", [128, 1024], F32)
    BK = c.sb("BK", [128, 2048], F32)
    T1S = c.sb("T1S", [128, 2048], F32)
    T2S = c.sb("T2S", [128, 2048], F32)
    CR4 = c.sb("CR4", [128, 1024], BF16)
    CI4 = c.sb("CI4", [128, 1024], BF16)
    KF4 = c.sb("KF4", [128, 2048], BF16)
    KF4s = c.sb("KF4s", [128, 2048], BF16)
    UF4 = c.sb("UF4", [128, 2048], BF16)
    PR4 = c.sb("PR4", [128, 1024], BF16)
    PI4 = c.sb("PI4", [128, 1024], BF16)
    SS4 = c.sb("SS4", [128, 2048], F32)
    RRT4 = c.sb("RRT4", [128, 1024], BF16)
    RIT4 = c.sb("RIT4", [128, 1024], BF16)
    spec_sb = c.sb("spec_sb", [128, 128, 128], BF16)

    TWO_PI = float(2 * np.pi)
    INV2PI = float(1.0 / (2 * np.pi))
    BIGC = float(1.5 * 2 ** 23)
    PIC = 3.1415925

    v.wait_ge(c._sems["ldc"], ldc_tok)
    for b in range(8):
        c.inc(v.tensor_copy(TS8[:, b * 128:(b + 1) * 128], TS[:]), "cv")
    ts8_tok = c.mile("cv")

    st = {"ub": [0, 0], "kru": [0, 0], "sprev": 0, "kc8": [0, 0],
          "lu": [0, 0], "pb03s": 0, "pb03v": 0, "pb47s": 0, "pb47v": 0,
          "t1w": 0, "t2w": 0, "cru": 0, "kfwv": 0, "kfwg": 0, "s1u": 0,
          "ssw_v": 0, "ssw_g": 0, "rrtu": 0}

    def load_u(gi):
        par = gi % 2
        gates = ([("ct", st["ub"][par])] if st["ub"][par] else []) + [
            ("ust", ust_total)]
        c.dma(g, f"lu{par}", UR[par][:],
              useq[gi * 8:(gi + 1) * 8, :].rearrange("c (i j) -> i c j", j=128),
              gates)
        c.dma(g, f"lu{par}", UI[par][:],
              useq[64 + gi * 8:64 + (gi + 1) * 8, :].rearrange(
                  "c (i j) -> i c j", j=128))
        st["lu"][par] = c.cnt[f"lu{par}"]

    def construct(gi):
        par = gi % 2
        KR8, KI8 = KR8b[par], KI8b[par]
        v.wait_ge(c._sems["cs"], st["sprev"])
        v.wait_ge(c._sems["ct"], st["kru"][par])
        v.wait_ge(c._sems["cv"], max(par_tok, et_tok, ts8_tok))
        for b in range(8):
            q = gi * 8 + b
            c.inc(v.scalar_tensor_tensor(out=X8[:, b * 128:(b + 1) * 128],
                                         in0=cs_["Tm0"][:],
                                         scalar=logdec[:, q:q + 1], in1=E_T[:],
                                         op0=ALU.mult, op1=ALU.add), "cv")
        x_tok = c.mile("cv")
        s.wait_ge(c._sems["cv"], x_tok)
        e1 = c.inc(s.activation(out=MM8[:], in_=X8[:], func=AF.Exp), "cs")
        for b in range(8):
            q = gi * 8 + b
            c.inc(v.tensor_scalar(out=PH8[:, b * 128:(b + 1) * 128],
                                  in0=cs_["Tm1"][:], scalar1=omega[:, q:q + 1],
                                  scalar2=None, op0=ALU.mult), "cv")
        c.mile("cv")
        c.inc(v.tensor_scalar(out=KK8[:], in0=PH8[:], scalar1=INV2PI,
                              scalar2=BIGC, op0=ALU.mult, op1=ALU.add), "cv")
        c.mile("cv")
        c.inc(v.tensor_scalar(out=KK8[:], in0=KK8[:], scalar1=BIGC, scalar2=None,
                              op0=ALU.subtract), "cv")
        c.mile("cv")
        c.inc(v.scalar_tensor_tensor(out=TH8[:], in0=KK8[:], scalar=-TWO_PI,
                                     in1=PH8[:], op0=ALU.mult, op1=ALU.add), "cv")
        c.mile("cv")
        th1 = c.inc(v.tensor_scalar(out=TH8[:], in0=TH8[:], scalar1=-PIC,
                                    scalar2=PIC, op0=ALU.max, op1=ALU.min), "cv")
        s.wait_ge(c._sems["cv"], th1)
        sn1 = c.inc(s.activation(out=SIN8[:], in_=TH8[:], func=AF.Sin), "cs")
        c.inc(v.tensor_scalar(out=PH8[:], in0=PH8[:], scalar1=float(np.pi / 2),
                              scalar2=None, op0=ALU.add), "cv")
        c.mile("cv")
        c.inc(v.tensor_scalar(out=KK8[:], in0=PH8[:], scalar1=INV2PI,
                              scalar2=BIGC, op0=ALU.mult, op1=ALU.add), "cv")
        c.mile("cv")
        c.inc(v.tensor_scalar(out=KK8[:], in0=KK8[:], scalar1=BIGC, scalar2=None,
                              op0=ALU.subtract), "cv")
        v.wait_ge(c._sems["cs"], sn1)
        c.mile("cv")
        c.inc(v.scalar_tensor_tensor(out=TH8[:], in0=KK8[:], scalar=-TWO_PI,
                                     in1=PH8[:], op0=ALU.mult, op1=ALU.add), "cv")
        c.mile("cv")
        th2 = c.inc(v.tensor_scalar(out=TH8[:], in0=TH8[:], scalar1=-PIC,
                                    scalar2=PIC, op0=ALU.max, op1=ALU.min), "cv")
        s.wait_ge(c._sems["cv"], th2)
        sn2 = c.inc(s.activation(out=COS8[:], in_=TH8[:], func=AF.Sin), "cs")
        st["sprev"] = sn2
        v.wait_ge(c._sems["cs"], e1)
        m_tok = c.inc(v.tensor_tensor(out=MM8[:], in0=MM8[:], in1=TS8[:],
                                      op=ALU.mult), "cv")
        v.wait_ge(c._sems["cs"], sn2)
        c.wait(v, "cv", m_tok)
        c.inc(v.tensor_tensor(out=KR8[:], in0=MM8[:], in1=COS8[:],
                              op=ALU.mult), "cv")
        kt_ = c.inc(v.tensor_tensor(out=KI8[:], in0=MM8[:], in1=SIN8[:],
                                    op=ALU.mult), "cv")
        st["kc8"][par] = kt_

    def bstage(par, cb, src_r, src_i, slc):
        t.wait_ge(c._sems["cs"], st["pb03s"])
        t.wait_ge(c._sems["cv"], st["pb03v"])
        for i in range(4):
            ar, ai = slc(src_r, src_i, cb + i)
            t.matmul(pb[i][:, 0:256], ar, cs_["W2r"][:], start=True, stop=False)
            c.inc(t.matmul(pb[i][:, 0:256], ai, cs_["W2in"][:], start=False,
                           stop=True), "ct")
            t.matmul(pb[i][:, 256:512], ar, cs_["W2i"][:], start=True, stop=False)
            c.inc(t.matmul(pb[i][:, 256:512], ai, cs_["W2r"][:], start=False,
                           stop=True), "ct")
        return c.cnt["ct"]

    def bk_evac(b_ct):
        s.wait_ge(c._sems["ct"], b_ct)
        s.wait_ge(c._sems["cv"], st["t1w"])
        s.wait_ge(c._sems["cg"], st["t2w"])
        for i in (0, 1):
            c.inc(s.activation(out=BK[:, i * 512:(i + 1) * 512],
                               in_=pb[i][:, 0:512], func=AF.Copy), "cs")
        bs = c.cnt["cs"]
        v.wait_ge(c._sems["ct"], b_ct)
        v.wait_ge(c._sems["cg"], st["t2w"])
        for i in (2, 3):
            c.inc(v.tensor_copy(BK[:, i * 512:(i + 1) * 512],
                                pb[i][:, 0:512]), "cv")
        bv = c.cnt["cv"]
        st["pb03s"], st["pb03v"] = bs, bv
        return bs, bv

    def fwd_twiddle(bs, bv):
        v.wait_ge(c._sems["cs"], bs)
        tv = c.inc(v.tensor_tensor(out=T1S[:], in0=BK[:], in1=cs_["CTA"][:],
                                   op=ALU.mult), "cv")
        g.wait_ge(c._sems["cs"], bs)
        g.wait_ge(c._sems["cv"], bv)
        tg = c.inc(g.tensor_tensor(out=T2S[:], in0=BK[:], in1=cs_["CTB"][:],
                                   op=ALU.mult), "cg")
        st["t1w"], st["t2w"] = tv, tg
        c.wait(v, "cv", tv)
        v.wait_ge(c._sems["ct"], st["cru"])
        for i in range(4):
            c.inc(v.tensor_tensor(out=CR4[:, i * 256:(i + 1) * 256],
                                  in0=T1S[:, i * 512:i * 512 + 256],
                                  in1=T1S[:, i * 512 + 256:i * 512 + 512],
                                  op=ALU.subtract), "cv")
        crt = c.mile("cv")
        c.wait(g, "cg", tg)
        g.wait_ge(c._sems["ct"], st["cru"])
        for i in range(4):
            c.inc(g.tensor_tensor(out=CI4[:, i * 256:(i + 1) * 256],
                                  in0=T2S[:, i * 512:i * 512 + 256],
                                  in1=T2S[:, i * 512 + 256:i * 512 + 512],
                                  op=ALU.add), "cg")
        cit = c.mile("cg")
        return crt, cit

    def dstage(crt, cit):
        t.wait_ge(c._sems["cv"], crt)
        t.wait_ge(c._sems["cg"], cit)
        t.wait_ge(c._sems["cs"], st["pb47s"])
        t.wait_ge(c._sems["cv"], st["pb47v"])
        for i in range(4):
            ks = slice(i * 256, (i + 1) * 256)
            t.matmul(pb[4 + i][:, 0:256], cs_["W1r"][:], CR4[:, ks],
                     start=True, stop=False)
            c.inc(t.matmul(pb[4 + i][:, 0:256], cs_["W1in"][:], CI4[:, ks],
                           start=False, stop=True), "ct")
            t.matmul(pb[4 + i][:, 256:512], cs_["W1i"][:], CR4[:, ks],
                     start=True, stop=False)
            c.inc(t.matmul(pb[4 + i][:, 256:512], cs_["W1r"][:], CI4[:, ks],
                           start=False, stop=True), "ct")
        st["cru"] = c.cnt["ct"]
        return st["cru"]

    def slc_k(src_r, src_i, ch):
        k = ch * 128
        return src_r[:, k:k + 128], src_i[:, k:k + 128]

    def slc_u(src_r, src_i, ch):
        return src_r[:, ch, :], src_i[:, ch, :]

    for gi in range(D_GROUPS):
        par = gi % 2
        load_u(gi)
        construct(gi)
        KR8, KI8 = KR8b[par], KI8b[par]
        for sg in range(D_QUADS):
            cb = sg * 4
            cbase = gi * 8 + sg * 4
            t.wait_ge(c._sems["cv"], st["kc8"][par])
            kb_ct = bstage(par, cb, KR8, KI8, slc_k)
            if sg == 1:
                st["kru"][par] = kb_ct
            if D_SUB < 2:
                continue
            bs, bv = bk_evac(kb_ct)
            if D_SUB < 3:
                continue
            crt, cit = fwd_twiddle(bs, bv)
            if D_SUB < 4:
                continue
            kd_ct = dstage(crt, cit)
            if D_SUB < 5:
                continue
            s.wait_ge(c._sems["ct"], kd_ct)
            s.wait_ge(c._sems["cv"], st["kfwv"])
            s.wait_ge(c._sems["cg"], st["kfwg"])
            for i in (0, 1):
                c.inc(s.activation(out=KF4[:, i * 512:(i + 1) * 512],
                                   in_=pb[4 + i][:, 0:512], func=AF.Copy), "cs")
            kf_s = c.cnt["cs"]
            v.wait_ge(c._sems["ct"], kd_ct)
            v.wait_ge(c._sems["cg"], st["kfwg"])
            for i in (2, 3):
                c.inc(v.tensor_copy(KF4[:, i * 512:(i + 1) * 512],
                                    pb[4 + i][:, 0:512]), "cv")
            kf_v = c.cnt["cv"]
            g.wait_ge(c._sems["cs"], kf_s)
            g.wait_ge(c._sems["cv"], kf_v)
            for i in range(4):
                c.inc(g.tensor_copy(KF4s[:, i * 512:i * 512 + 256],
                                    KF4[:, i * 512 + 256:i * 512 + 512]), "cg")
                c.inc(g.tensor_copy(KF4s[:, i * 512 + 256:i * 512 + 512],
                                    KF4[:, i * 512:i * 512 + 256]), "cg")
            kfs_g = c.cnt["cg"]
            st["pb47s"], st["pb47v"] = kf_s, kf_v
            if D_STAGE < 2:
                continue
            t.wait_ge(c._sems[f"lu{par}"], st["lu"][par])
            ub_ct = bstage(par, cb, UR[par], UI[par], slc_u)
            if sg == 1:
                st["ub"][par] = ub_ct
            bs2, bv2 = bk_evac(ub_ct)
            crt2, cit2 = fwd_twiddle(bs2, bv2)
            ud_ct = dstage(crt2, cit2)
            s.wait_ge(c._sems["ct"], ud_ct)
            s.wait_ge(c._sems["cv"], st["kfwv"])
            s.wait_ge(c._sems["cg"], st["kfwg"])
            for i in (0, 1):
                c.inc(s.activation(out=UF4[:, i * 512:(i + 1) * 512],
                                   in_=pb[4 + i][:, 0:512], func=AF.Copy), "cs")
            uf_s = c.cnt["cs"]
            v.wait_ge(c._sems["ct"], ud_ct)
            v.wait_ge(c._sems["cg"], st["kfwg"])
            for i in (2, 3):
                c.inc(v.tensor_copy(UF4[:, i * 512:(i + 1) * 512],
                                    pb[4 + i][:, 0:512]), "cv")
            uf_v = c.cnt["cv"]
            st["pb47s"], st["pb47v"] = uf_s, uf_v
            if D_STAGE < 3:
                continue
            v.wait_ge(c._sems["cs"], max(kf_s, uf_s))
            pv = c.inc(v.tensor_tensor(out=T1S[:], in0=UF4[:], in1=KF4[:],
                                       op=ALU.mult), "cv")
            g.wait_ge(c._sems["cs"], max(kf_s, uf_s))
            g.wait_ge(c._sems["cv"], max(kf_v, uf_v))
            pg = c.inc(g.tensor_tensor(out=T2S[:], in0=UF4[:], in1=KF4s[:],
                                       op=ALU.mult), "cg")
            st["t1w"], st["t2w"] = pv, pg
            st["kfwv"], st["kfwg"] = pv, pg
            c.wait(v, "cv", pv)
            v.wait_ge(c._sems["ct"], st["s1u"])
            for i in range(4):
                c.inc(v.tensor_tensor(out=PR4[:, i * 256:(i + 1) * 256],
                                      in0=T1S[:, i * 512:i * 512 + 256],
                                      in1=T1S[:, i * 512 + 256:i * 512 + 512],
                                      op=ALU.subtract), "cv")
            pr_tok = c.mile("cv")
            c.wait(g, "cg", pg)
            g.wait_ge(c._sems["ct"], st["s1u"])
            for i in range(4):
                c.inc(g.tensor_tensor(out=PI4[:, i * 256:(i + 1) * 256],
                                      in0=T2S[:, i * 512:i * 512 + 256],
                                      in1=T2S[:, i * 512 + 256:i * 512 + 512],
                                      op=ALU.add), "cg")
            pi_tok = c.mile("cg")
            if D_STAGE < 4:
                continue
            t.wait_ge(c._sems["cv"], pr_tok)
            t.wait_ge(c._sems["cg"], pi_tok)
            t.wait_ge(c._sems["cs"], st["pb03s"])
            t.wait_ge(c._sems["cv"], st["pb03v"])
            for ch in range(4):
                for chk in range(2):
                    prs = PR4[:, ch * 256 + chk * 128:ch * 256 + (chk + 1) * 128]
                    pis = PI4[:, ch * 256 + chk * 128:ch * 256 + (chk + 1) * 128]
                    t.matmul(pb[ch][:, chk * 256:chk * 256 + 128], prs,
                             cs_["V128r"][:], start=True, stop=False)
                    c.inc(t.matmul(pb[ch][:, chk * 256:chk * 256 + 128], pis,
                                   cs_["V128in"][:], start=False, stop=True), "ct")
                    t.matmul(pb[ch][:, chk * 256 + 128:chk * 256 + 256], prs,
                             cs_["V128i"][:], start=True, stop=False)
                    c.inc(t.matmul(pb[ch][:, chk * 256 + 128:chk * 256 + 256], pis,
                                   cs_["V128r"][:], start=False, stop=True), "ct")
            s1_ct = c.cnt["ct"]
            st["s1u"] = s1_ct
            s.wait_ge(c._sems["ct"], s1_ct)
            s.wait_ge(c._sems["cv"], st["ssw_v"])
            s.wait_ge(c._sems["cg"], st["ssw_g"])
            for i in (0, 1):
                c.inc(s.activation(out=SS4[:, i * 512:(i + 1) * 512],
                                   in_=pb[i][:, 0:512], func=AF.Copy), "cs")
            ss_s = c.cnt["cs"]
            v.wait_ge(c._sems["ct"], s1_ct)
            v.wait_ge(c._sems["cg"], st["ssw_g"])
            for i in (2, 3):
                c.inc(v.tensor_copy(SS4[:, i * 512:(i + 1) * 512],
                                    pb[i][:, 0:512]), "cv")
            ss_v = c.cnt["cv"]
            st["pb03s"], st["pb03v"] = ss_s, ss_v
            v.wait_ge(c._sems["cs"], ss_s)
            t1b = c.inc(v.tensor_tensor(out=T1S[:], in0=SS4[:], in1=cs_["CT2A"][:],
                                        op=ALU.mult), "cv")
            g.wait_ge(c._sems["cs"], ss_s)
            g.wait_ge(c._sems["cv"], ss_v)
            t2b = c.inc(g.tensor_tensor(out=T2S[:], in0=SS4[:], in1=cs_["CT2B"][:],
                                        op=ALU.mult), "cg")
            st["t1w"], st["t2w"] = t1b, t2b
            st["ssw_v"], st["ssw_g"] = t1b, t2b
            c.wait(v, "cv", t1b)
            v.wait_ge(c._sems["ct"], st["rrtu"])
            for ch in range(4):
                for chk in range(2):
                    base = ch * 512 + chk * 256
                    c.inc(v.tensor_tensor(
                        out=RRT4[:, (ch * 2 + chk) * 128:(ch * 2 + chk + 1) * 128],
                        in0=T1S[:, base:base + 128],
                        in1=T1S[:, base + 128:base + 256], op=ALU.subtract), "cv")
            rrt_tok = c.mile("cv")
            c.wait(g, "cg", t2b)
            g.wait_ge(c._sems["ct"], st["rrtu"])
            for ch in range(4):
                for chk in range(2):
                    base = ch * 512 + chk * 256
                    c.inc(g.tensor_tensor(
                        out=RIT4[:, (ch * 2 + chk) * 128:(ch * 2 + chk + 1) * 128],
                        in0=T2S[:, base:base + 128],
                        in1=T2S[:, base + 128:base + 256], op=ALU.add), "cg")
            rit_tok = c.mile("cg")
            if D_STAGE < 5:
                continue
            t.wait_ge(c._sems["cv"], rrt_tok)
            t.wait_ge(c._sems["cg"], rit_tok)
            t.wait_ge(c._sems["cs"], st["pb47s"])
            t.wait_ge(c._sems["cv"], st["pb47v"])
            for ch in range(4):
                r0 = RRT4[:, (ch * 2) * 128:(ch * 2 + 1) * 128]
                r1 = RRT4[:, (ch * 2 + 1) * 128:(ch * 2 + 2) * 128]
                i0 = RIT4[:, (ch * 2) * 128:(ch * 2 + 1) * 128]
                i1 = RIT4[:, (ch * 2 + 1) * 128:(ch * 2 + 2) * 128]
                t.matmul(pb[4 + ch][:, 0:128], cs_["Q256r0"][:], r0,
                         start=True, stop=False)
                t.matmul(pb[4 + ch][:, 0:128], cs_["Q256in0"][:], i0,
                         start=False, stop=False)
                t.matmul(pb[4 + ch][:, 0:128], cs_["Q256r1"][:], r1,
                         start=False, stop=False)
                c.inc(t.matmul(pb[4 + ch][:, 0:128], cs_["Q256in1"][:], i1,
                               start=False, stop=True), "ct")
                t.matmul(pb[4 + ch][:, 128:256], cs_["Q256i0"][:], r0,
                         start=True, stop=False)
                t.matmul(pb[4 + ch][:, 128:256], cs_["Q256r0"][:], i0,
                         start=False, stop=False)
                t.matmul(pb[4 + ch][:, 128:256], cs_["Q256i1"][:], r1,
                         start=False, stop=False)
                c.inc(t.matmul(pb[4 + ch][:, 128:256], cs_["Q256r1"][:], i1,
                               start=False, stop=True), "ct")
            s2_ct = c.cnt["ct"]
            st["rrtu"] = s2_ct
            s.wait_ge(c._sems["ct"], s2_ct)
            for ch in range(4):
                c.inc(s.activation(out=spec_sb[:, cbase + ch, :],
                                   in_=pb[4 + ch][:, 0:128], func=AF.Copy), "cs")
                c.inc(s.activation(out=spec_sb[:, 64 + cbase + ch, :],
                                   in_=pb[4 + ch][:, 128:256], func=AF.Copy), "cs")
            st["pb47s"] = c.cnt["cs"]
            st["pb47v"] = c.cnt["cv"]
    g.wait_ge(c._sems["cs"], c.cnt["cs"])
    g.wait_ge(c._sems["cv"], c.cnt["cv"])
    for fb in range(4):
        c.dma(g, "stsp", spec_o.rearrange("f (i j) -> i f j", j=128)[:, fb * 32:(fb + 1) * 32, :],
              spec_sb[:, fb * 32:(fb + 1) * 32, :])
    g.wait_ge(c._sems["stsp"], c.cnt["stsp"])
    c.cur_es.close()
    c.cur_es = None


def phase_E(c, io):
    from concourse.bass_core_rust import IndirectOffsetOnAxis
    v, s, t, g, sy = c.v, c.s, c.t, c.g, c.sy
    c.prefix = "E_"
    c.cur_es = ExitStack()
    x1 = io["x1d"]
    ymoe = io["ymoe_d"]
    spec_g = io["spec_g"]
    sidx = io["sidx"]
    cp = io["cp_d"]
    gatei = io["gate_d"]
    wfst = io["wfst"]
    oscal = io["oscal"]
    n2g = io["n2g"]
    n2b = io["n2b"]
    w1t = io["w1t"]
    b1 = io["eb1"]
    w2t = io["w2t"]
    b2 = io["eb2"]
    out_o = io["out_o"]

    pj = [c.ps(f"pj{j}", [128, 512], F32) for j in range(2)]
    pk = [c.ps(f"pk{j}", [128, 512], F32) for j in range(2)]
    tp = [c.ps(f"tp{j}", [128, 128], BF16) for j in range(4)]

    b1c, tok_b1 = c.bias_col(b1, H, "b1c", pj[0])
    onesr = c.sb("onesr", [1, 128], F32)
    c.inc(v.memset(onesr[:], 1.0), "cv")
    ones_tok = c.mile("cv")
    reps = {}
    rows_d = {"g2": n2g, "b2n": n2b, "b2f": b2}
    t.wait_ge(c._sems["cv"], max(ones_tok, tok_b1))
    rep_ev = 0
    for nm, vec in rows_d.items():
        row = c.sb(nm + "_row", [1, D], F32)
        rep = c.sb(nm + "_rep", [128, D], F32)
        lr = c.dma(sy, "lr_" + nm, row[:], vec.rearrange("(a b) -> a b", a=1))
        t.wait_ge(c._sems["lr_" + nm], lr)
        for hh in range(2):
            if rep_ev:
                t.wait_ge(c._sems["cv"], rep_ev)
            mm = t.matmul(pk[0][:], onesr[:], row[:, hh * 512:(hh + 1) * 512],
                          start=True, stop=True)
            tk = c.inc(mm, "ct")
            c.wait(v, "ct", tk)
            rep_ev = c.inc(v.tensor_copy(rep[:, hh * 512:(hh + 1) * 512],
                                         pk[0][:]), "cv")
        reps[nm] = rep
    osc = c.sb("osc", [128, 1], F32)
    osrow = c.sb("osrow", [1, 1], F32)
    lo = c.dma(sy, "lros", osrow[:], oscal[:])
    t.wait_ge(c._sems["lros"], lo)
    t.wait_ge(c._sems["cv"], rep_ev)
    mm = t.matmul(pk[0][:, 0:1], onesr[:], osrow[:], start=True, stop=True)
    tk = c.inc(mm, "ct")
    c.wait(v, "ct", tk)
    c.inc(v.tensor_copy(osc[:], pk[0][:, 0:1]), "cv")
    rep_tok = c.mile("cv")

    x2sb = c.sb("x2sb", [128, NT, D], F32)

    esA = ExitStack()
    spec_sb = c.sb("spec_sb", [128, DC, TC], BF16, esA)
    yg_fm = c.sb("yg_fm", [128, DC, TC], BF16, esA)
    wfs_sb = c.sb("wfs_sb", [128, DC, D], BF16, esA)
    cg_st = [c.sb(f"cg_st{j}", [128, 512], BF16, esA) for j in range(2)]
    cp_st = [c.sb(f"cp_st{j}", [128, 512], BF16, esA) for j in range(2)]
    t1s = [c.sb(f"t1s{j}", [128, 512], F32, esA) for j in range(2)]
    sidx_sb = c.sb("sidx_sb", [128, 8], mybir.dt.int32, esA)
    c.dma(sy, "sixl", sidx_sb[:], sidx[:])
    g.wait_ge(c._sems["sixl"], c.cnt["sixl"])
    spec_v = spec_g.rearrange("f (r t) -> (f r) t", r=NCORE)
    if "lsp" not in c._sems:
        c.sem("lsp")
    for a in range(DC):
        g.indirect_dma_start(spec_sb[:, a, :], None, spec_v[:],
                             IndirectOffsetOnAxis(sidx_sb[:, a:a + 1], 0)
                             ).then_inc(c._sems["lsp"], 16)
        c.cnt["lsp"] += 16
    lsp = c.cnt["lsp"]
    lwf = c.dma(g, "lwf", wfs_sb[:], wfst.rearrange("(a p) n -> p a n", p=128))
    t.wait_ge(c._sems["lsp"], lsp)
    t.wait_ge(c._sems["lwf"], lwf)
    pj_free = [c.cnt["cs"]] * 2
    ld_free = [0, 0]
    for oc in range(DC):
        for qb in range(TC // 512):
            bank = (oc * (TC // 512) + qb) % 2
            lg1 = c.dma(sy, f"lcg{bank}", cg_st[bank][:],
                        gatei[oc * 128:(oc + 1) * 128, qb * 512:(qb + 1) * 512],
                        gates=[("cv", ld_free[bank])])
            lg2 = c.dma(sy, f"lcp{bank}", cp_st[bank][:],
                        cp[oc * 128:(oc + 1) * 128, qb * 512:(qb + 1) * 512],
                        gates=[("cv", ld_free[bank])])
            t.wait_ge(c._sems["cs"], pj_free[bank])
            for dc in range(DC):
                mm = t.matmul(pj[bank][:], wfs_sb[:, dc, oc * 128:(oc + 1) * 128],
                              spec_sb[:, dc, qb * 512:(qb + 1) * 512],
                              start=(dc == 0), stop=(dc == DC - 1))
            mt = c.inc(mm, "ct")
            s.wait_ge(c._sems["ct"], mt)
            s.wait_ge(c._sems["cv"], max(rep_tok, ld_free[bank]))
            ep = c.inc(s.activation(out=t1s[bank][:], in_=pj[bank][:],
                                    func=AF.Identity, scale=osc[:]), "cs")
            pj_free[bank] = ep
            v.wait_ge(c._sems["cs"], ep)
            v.wait_ge(c._sems[f"lcg{bank}"], lg1)
            v.wait_ge(c._sems[f"lcp{bank}"], lg2)
            e1 = c.inc(v.tensor_tensor(out=t1s[bank][:], in0=t1s[bank][:],
                                       in1=cg_st[bank][:], op=ALU.mult), "cv")
            c.wait(v, "cv", e1)
            e2 = c.inc(v.tensor_tensor(out=t1s[bank][:], in0=t1s[bank][:],
                                       in1=cp_st[bank][:], op=ALU.add), "cv")
            c.wait(v, "cv", e2)
            e3 = c.inc(v.tensor_scalar(out=yg_fm[:, oc, qb * 512:(qb + 1) * 512],
                                       in0=t1s[bank][:], scalar1=0.5, scalar2=None,
                                       op0=ALU.mult), "cv")
            ld_free[bank] = c.mile("cv")
    yg_tok = c.mile("cv")

    xt = [c.sb(f"xt{j}", [128, D], F32, esA) for j in range(2)]
    scr = c.sb("scr", [128, D], F32, esA)
    ygt = c.sb("ygt", [128, D], BF16, esA)
    sx = c.sb("sx", [128, 1], F32, esA)
    ssq = c.sb("ssq", [128, 1], F32, esA)
    mu = c.sb("mu", [128, 1], F32, esA)
    musq = c.sb("musq", [128, 1], F32, esA)
    var = c.sb("var", [128, 1], F32, esA)
    rstd = c.sb("rstd", [128, 1], F32, esA)
    eps = c.sb("eps", [128, 1], F32, esA)
    junk = c.sb("junk", [128, D], BF16, esA)
    c.inc(v.memset(eps[:], 1e-5), "cv")
    c.mile("cv")

    def ln_stats(xi, ldsem=None, ldt=0, gate_cv=0):
        if ldsem:
            s.wait_ge(c._sems[ldsem], ldt)
        if gate_cv:
            s.wait_ge(c._sems["cv"], gate_cv)
        t1 = c.inc(s.activation(out=junk[:], in_=xi[:], func=AF.Square,
                                accum_out=ssq[:]), "cs")
        if ldsem:
            v.wait_ge(c._sems[ldsem], ldt)
        t2 = c.inc(v.tensor_scalar(out=scr[:], in0=xi[:], scalar1=0.0, scalar2=None,
                                   op0=ALU.add, op1=ALU.add, accum_out=sx[:]), "cv")
        c.wait(v, "cv", t2)
        t3 = c.inc(v.tensor_scalar(out=mu[:], in0=sx[:], scalar1=1.0 / D,
                                   scalar2=None, op0=ALU.mult), "cv")
        c.wait(v, "cv", t3)
        t4 = c.inc(v.tensor_tensor(out=musq[:], in0=mu[:], in1=mu[:], op=ALU.mult),
                   "cv")
        c.wait(v, "cs", t1)
        c.wait(v, "cv", t4)
        t5 = c.inc(v.scalar_tensor_tensor(out=var[:], in0=ssq[:], scalar=1.0 / D,
                                          in1=musq[:], op0=ALU.mult,
                                          op1=ALU.subtract), "cv")
        c.wait(s, "cv", t5)
        t6 = c.inc(s.activation(out=rstd[:], in_=var[:], func=AF.Ln, bias=eps[:]),
                   "cs")
        c.wait(s, "cs", t6)
        t7 = c.inc(s.activation(out=rstd[:], in_=rstd[:], func=AF.Exp, scale=-0.5),
                   "cs")
        c.wait(v, "cs", t7)

    tp_free = [0, 0, 0, 0]
    tile_done = [0] * NT
    for i in range(NT):
        lxs = f"lx{i % 2}"
        gates = [("cv", tile_done[i - 2])] if i >= 2 else []
        ldt = c.dma(sy, lxs, xt[i % 2][:], x1[i * 128:(i + 1) * 128, :], gates)
        xi = xt[i % 2]
        ln_stats(xi, lxs, ldt)
        t8 = c.inc(v.tensor_scalar(out=scr[:], in0=xi[:], scalar1=mu[:],
                                   scalar2=rstd[:], op0=ALU.subtract,
                                   op1=ALU.mult), "cv")
        tile_done[i] = t8
        v.wait_ge(c._sems["cv"], yg_tok)
        for cc in range(DC):
            j = cc % 4
            t.wait_ge(c._sems["cv"], max(tp_free[j], yg_tok))
            tt = c.inc(t.transpose(tp[j][:], yg_fm[:, cc, i * 128:(i + 1) * 128],
                                   c.ident_bf[:]), "ct")
            c.wait(v, "ct", tt)
            ev = c.inc(v.tensor_copy(ygt[:, cc * 128:(cc + 1) * 128], tp[j][:]),
                       "cv")
            tp_free[j] = ev
        yt_tok = c.mile("cv")
        a1_ = c.inc(v.tensor_tensor(out=x2sb[:, i, :], in0=scr[:], in1=ygt[:],
                                    op=ALU.add), "cv")
        c.mile("cv")
    x2_tok = c.mile("cv")
    g.wait_ge(c._sems["cv"], x2_tok)
    g.wait_ge(c._sems["ct"], c.cnt["ct"])
    g.wait_ge(c._sems["cs"], c.cnt["cs"])
    ph1_done = (("cv", x2_tok), ("ct", c.cnt["ct"]), ("cs", c.cnt["cs"]))
    esA.close()

    xn2q2 = [c.sb(f"xn2q{j}", [128, DC, 512], BF16) for j in range(2)]
    w1t_sb = [c.sb(f"w1t_sb{j}", [128, DC, 512], BF16) for j in range(2)]
    w2t_sb = [c.sb(f"w2t_sb{j}", [128, HC, 128], BF16) for j in range(2)]
    h_bf = c.sb("h_bf", [128, HC, 512], BF16)
    mfm = c.sb("mfm", [128, DC, 512], BF16)
    xtb = [c.sb("xtb0", [128, D], F32)] * 2
    ymr = [c.sb("ymr0", [128, D], F32)] * 2
    otm = [c.sb("otm0", [128, D], F32)] * 2
    scr2 = c.sb("scr2", [128, D], F32)
    scrb2 = [c.sb(f"scrb{j}", [128, D], BF16) for j in range(2)]
    sx2 = c.sb("sx2", [128, 1], F32)
    ssq2 = c.sb("ssq2", [128, 1], F32)
    mu2 = c.sb("mu2", [128, 1], F32)
    musq2 = c.sb("musq2", [128, 1], F32)
    var2 = c.sb("var2", [128, 1], F32)
    rstd2 = c.sb("rstd2", [128, 1], F32)
    eps2 = c.sb("eps2", [128, 1], F32)
    junk2 = c.sb("junk2", [128, D], BF16)
    c.inc(v.memset(eps2[:], 1e-5), "cv")
    c.mile("cv")
    s.wait_ge(c._sems["cv"], x2_tok)

    def ln2_stats(xi):
        t1 = c.inc(s.activation(out=junk2[:], in_=xi, func=AF.Square,
                                accum_out=ssq2[:]), "cs")
        t2 = c.inc(v.tensor_scalar(out=scr2[:], in0=xi, scalar1=0.0, scalar2=None,
                                   op0=ALU.add, op1=ALU.add, accum_out=sx2[:]), "cv")
        c.wait(v, "cv", t2)
        t3 = c.inc(v.tensor_scalar(out=mu2[:], in0=sx2[:], scalar1=1.0 / D,
                                   scalar2=None, op0=ALU.mult), "cv")
        c.wait(v, "cv", t3)
        t4 = c.inc(v.tensor_tensor(out=musq2[:], in0=mu2[:], in1=mu2[:],
                                   op=ALU.mult), "cv")
        c.wait(v, "cs", t1)
        c.wait(v, "cv", t4)
        t5 = c.inc(v.scalar_tensor_tensor(out=var2[:], in0=ssq2[:], scalar=1.0 / D,
                                          in1=musq2[:], op0=ALU.mult,
                                          op1=ALU.subtract), "cv")
        c.wait(s, "cv", t5)
        t6 = c.inc(s.activation(out=rstd2[:], in_=var2[:], func=AF.Ln, bias=eps2[:]),
                   "cs")
        c.wait(s, "cs", t6)
        t7 = c.inc(s.activation(out=rstd2[:], in_=rstd2[:], func=AF.Exp, scale=-0.5),
                   "cs")
        c.wait(v, "cs", t7)

    prev_mlp_ct = 0
    ot_st = [0, 0]
    w2tok = [0, 0]
    w2_used = [0, 0]
    scrb_ct = [0, 0]
    xn2_toks = [0, 0]
    w1tok_buf = [0, 0]
    w1_used_ct = [0, 0]

    def emit_ln(qb):
        for tsub in range(4):
            i = qb * 4 + tsub
            sp_ = tsub % 2
            ln2_stats(x2sb[:, i, :])
            u1 = c.inc(v.tensor_scalar(out=scr2[:], in0=x2sb[:, i, :], scalar1=mu2[:],
                                       scalar2=rstd2[:], op0=ALU.subtract,
                                       op1=ALU.mult), "cv")
            c.wait(v, "cv", u1)
            v.wait_ge(c._sems["cv"], rep_tok)
            u2 = c.inc(v.tensor_tensor(out=scr2[:], in0=scr2[:], in1=reps["g2"][:],
                                       op=ALU.mult), "cv")
            c.wait(v, "cv", u2)
            u3 = c.inc(v.tensor_tensor(out=scr2[:], in0=scr2[:], in1=reps["b2n"][:],
                                       op=ALU.add), "cv")
            c.wait(v, "cv", u3)
            v.wait_ge(c._sems["ct"], scrb_ct[sp_])
            u4 = c.inc(v.tensor_copy(scrb2[sp_][:], scr2[:]), "cv")
            uk = c.mile("cv")
            for cc in range(DC):
                j = cc % 4
                t.wait_ge(c._sems["cv"], max(tp_free[j], uk))
                tt = c.inc(t.transpose(tp[j][:],
                                       scrb2[sp_][:, cc * 128:(cc + 1) * 128],
                                       c.ident_bf[:]), "ct")
                c.wait(v, "ct", tt)
                ev = c.inc(v.tensor_copy(
                    xn2q2[qb % 2][:, cc, tsub * 128:(tsub + 1) * 128],
                    tp[j][:]), "cv")
                tp_free[j] = ev
            scrb_ct[sp_] = c.cnt["ct"]
        xn2_toks[qb % 2] = c.mile("cv")

    def load_w1(qb, quar, first=False):
        w1tok_buf[quar % 2] = c.dma(
            g, "lw1t", w1t_sb[quar % 2][:],
            w1t.rearrange("(a p) n -> p a n", p=128)[:, :, quar * 512:(quar + 1) * 512],
            gates=list(ph1_done) if first else [("ct", w1_used_ct[quar % 2])])

    emit_ln(0)
    for qb in range(TC // 512):
        par = qb % 2
        if qb == 0:
            load_w1(qb, 0, first=True)
            load_w1(qb, 1, first=True)
        pq_free = [c.cnt["cs"]] * 2
        t.wait_ge(c._sems["cv"], xn2_toks[par])
        for quar in range(8):
            t.wait_ge(c._sems["lw1t"], w1tok_buf[quar % 2])
            for hc in range(4):
                hg = quar * 4 + hc
                bank = hg % 2
                t.wait_ge(c._sems["cs"], pq_free[bank])
                for dc in range(DC):
                    mm = t.matmul(pj[bank][:],
                                  w1t_sb[quar % 2][:, dc, hc * 128:(hc + 1) * 128],
                                  xn2q2[par][:, dc, :],
                                  start=(dc == 0), stop=(dc == DC - 1))
                mt = c.inc(mm, "ct")
                s.wait_ge(c._sems["ct"], mt)
                s.wait_ge(c._sems["cv"], tok_b1)
                ep = c.inc(s.activation(out=h_bf[:, hg, :], in_=pj[bank][:],
                                        func=AF.Gelu, bias=b1c[:, hg:hg + 1]), "cs")
                pq_free[bank] = ep
            w1_used_ct[quar % 2] = c.cnt["ct"]
            nq, nqb = quar + 2, qb
            if nq >= 8:
                nq, nqb = nq - 8, qb + 1
            if nqb < TC // 512:
                load_w1(nqb, nq)
        h_tok = c.mile("cs")
        if qb + 1 < TC // 512:
            emit_ln(qb + 1)
        t.wait_ge(c._sems["cs"], h_tok)
        py_free = [c.cnt["cs"]] * 2
        for oc in range(DC):
            bank = oc % 2
            w2tok[bank] = c.dma(g, f"lw2t{bank}", w2t_sb[bank][:],
                                w2t.rearrange("(a p) n -> p a n", p=128)[:, :, oc * 128:(oc + 1) * 128],
                                gates=[("ct", w2_used[bank])] if w2_used[bank] else list(ph1_done))
            t.wait_ge(c._sems[f"lw2t{bank}"], w2tok[bank])
            t.wait_ge(c._sems["cs"], py_free[bank])
            for hcc in range(HC):
                mm = t.matmul(pk[bank][:], w2t_sb[bank][:, hcc, :],
                              h_bf[:, hcc, :], start=(hcc == 0), stop=(hcc == HC - 1))
            mt = c.inc(mm, "ct")
            w2_used[bank] = mt
            s.wait_ge(c._sems["ct"], mt)
            if prev_mlp_ct:
                s.wait_ge(c._sems["ct"], prev_mlp_ct)
            ep = c.inc(s.activation(out=mfm[:, oc, :], in_=pk[bank][:],
                                    func=AF.Copy), "cs")
            py_free[bank] = ep
        m_tok = c.mile("cs")
        for tsub in range(4):
            jb = 0
            i = qb * 4 + tsub
            son = f"so{jb}"
            lx2 = c.dma(sy, f"lxf{jb}", xtb[jb][:], x1[i * 128:(i + 1) * 128, :],
                        gates=[(son, ot_st[jb])] if son in c._sems else list(ph1_done))
            lym = c.dma(sy, f"lyf{jb}", ymr[jb][:], ymoe[i * 128:(i + 1) * 128, :],
                        gates=[(son, ot_st[jb])] if son in c._sems else [])
            otmp = otm[jb]
            for cc in range(DC):
                j = cc % 4
                t.wait_ge(c._sems["cv"], tp_free[j])
                t.wait_ge(c._sems["cs"], m_tok)
                tt = c.inc(t.transpose(tp[j][:], mfm[:, cc, tsub * 128:(tsub + 1) * 128],
                                       c.ident_bf[:]), "ct")
                c.wait(v, "ct", tt)
                if cc == 0 and son in c._sems:
                    v.wait_ge(c._sems[son], ot_st[jb])
                ev = c.inc(v.tensor_copy(otmp[:, cc * 128:(cc + 1) * 128], tp[j][:]),
                           "cv")
                tp_free[j] = ev
            c.mile("cv")
            f1 = c.inc(v.tensor_tensor(out=otmp[:], in0=otmp[:], in1=reps["b2f"][:],
                                       op=ALU.add), "cv")
            c.wait(v, "cv", f1)
            f2 = c.inc(v.tensor_tensor(out=otmp[:], in0=otmp[:], in1=x2sb[:, i, :],
                                       op=ALU.add), "cv")
            c.wait(v, "cv", f2)
            v.wait_ge(c._sems[f"lxf{jb}"], lx2)
            f3 = c.inc(v.tensor_tensor(out=otmp[:], in0=otmp[:], in1=xtb[jb][:],
                                       op=ALU.add), "cv")
            c.wait(v, "cv", f3)
            v.wait_ge(c._sems[f"lyf{jb}"], lym)
            f4 = c.inc(v.tensor_tensor(out=otmp[:], in0=otmp[:], in1=ymr[jb][:],
                                       op=ALU.add), "cv")
            c.mile("cv")
            ot_st[jb] = c.dma(g, son, out_o[i * 128:(i + 1) * 128, :], otmp[:],
                              gates=[("cv", f4)])
        prev_mlp_ct = c.cnt["ct"]
    for nm in ["so0", "so1"]:
        if nm in c._sems:
            g.wait_ge(c._sems[nm], c.cnt[nm])
    c.cur_es.close()
    c.cur_es = None


def host_consts():
    ident = np.eye(128, dtype=np.float32)
    return {"identbf": ident.astype(ml_dtypes.bfloat16), "identf32": ident}



_CACHE = {}


def build_fused():
    c = Ctx()
    io = {}
    din, dint, dout = c.din, c.dint, c.dout
    io["x"] = din("x", [TC, D], F32)
    io["mtm"] = din("mtm", [TC, D], BF16)
    for nm in ["w1", "w2", "w3", "w4"]:
        io[nm] = din(nm, [D, D], F32)
    for nm in ["b1", "b2", "b3"]:
        io[nm] = din(nm, [D], F32)
    io["b4"] = din("b4", [1, D], F32)
    io["fw1t"] = din("fw1t", [D, H], F32)
    io["fw2t"] = din("fw2t", [H, D], F32)
    io["fb1"] = din("fb1", [H], F32)
    io["fb2"] = din("fb2", [D], F32)
    io["norm_g"] = din("norm_g", [D], F32)
    io["norm_b"] = din("norm_b", [D], F32)
    io["wct"] = din("wct", [D, D], F32)
    io["bcon"] = din("bcon", [D], F32)
    io["wcgt"] = din("wcgt", [D, D], F32)
    io["bcg"] = din("bcg", [D], F32)
    io["wgt"] = din("wgt", [D, D], F32)
    io["bgate"] = din("bgate", [D], F32)
    io["wspt"] = din("wspt", [D, D], F32)
    io["wfgt"] = din("wfgt", [D, NF], F32)
    io["bfg"] = din("bfg", [NF], F32)
    io["wtgt"] = din("wtgt", [D, 1], F32)
    io["btg"] = din("btg", [1, 1], F32)
    io["wbandt"] = din("wbandt", [D, 4], F32)
    io["bband"] = din("bband", [1, 4], F32)
    io["wgmoet"] = din("wgmoet", [D, 8], F32)
    io["lnsc"] = din("lnsc", [8, D], F32)
    io["lnbi"] = din("lnbi", [8, D], F32)
    io["w1e"] = din("w1e", [8, D, 256], F32)
    io["selrep"] = din("selrep", [8, 8 * 128], BF16)
    io["b1e"] = din("b1e", [8, 256], F32)
    io["w2e"] = din("w2e", [8, 256, D], F32)
    io["b2e"] = din("b2e", [8, D], F32)
    io["ld_c"] = din("ld_c", [1, CPC], F32)
    io["fi_c"] = din("fi_c", [1, CPC], F32)
    io["fi_full"] = din("fi_full", [1, NF], F32)
    io["fr_c"] = din("fr_c", [1, CPC], F32)
    io["dt_c"] = din("dt_c", [1, CPC], F32)
    io["uidx"] = din("uidx", [128, 8], mybir.dt.int32)
    io["sidx"] = din("sidx", [128, 8], mybir.dt.int32)
    io["bsel"] = din("bsel", [128, 4], F32)
    for nm, (shape, dt_) in FFT_CONST_SPECS.items():
        io[nm] = din(nm, shape, dt_)
    io["wfst"] = din("wfst", [D, D], F32)
    io["oscal"] = din("oscal", [1, 1], F32)
    io["n2g"] = din("n2g", [D], F32)
    io["n2b"] = din("n2b", [D], F32)
    io["w1t"] = din("w1t", [D, H], F32)
    io["eb1"] = din("eb1", [H], F32)
    io["w2t"] = din("w2t", [H, D], F32)
    io["eb2"] = din("eb2", [D], F32)

    io["qf"] = dint("qf", [D, TC], BF16)
    io["kf"] = dint("kf", [D, TC], BF16)
    io["vw4"] = dint("vw4", [TC, D], BF16)
    io["y2o"] = dint("y2o", [TC, D], BF16)
    io["x1d"] = dint("x1d", [TC, D], F32)
    io["u_d"] = dint("u_d", [D, TC], BF16)
    io["aux_d"] = dint("aux_d", [5, TC], F32)
    io["cp_d"] = dint("cp_d", [D, TC], BF16)
    io["gate_d"] = dint("gate_d", [D, TC], BF16)
    io["ymoe_d"] = dint("ymoe_d", [TC, D], F32)
    io["useq_d"] = dint("useq_d", [128, T], BF16)
    io["spec_d"] = dint("spec_d", [128, T], BF16)
    io["kf_g"] = dint("kf_g", [NCORE * D, TC], BF16)
    io["vw4_g"] = dint("vw4_g", [T, D], BF16)
    io["u_g"] = dint("u_g", [NCORE * D, TC], BF16)
    io["aux_g"] = dint("aux_g", [NCORE * 5, TC], F32)
    io["spec_g"] = dint("spec_g", [NCORE * 128, T], BF16)
    io["out_o"] = dout("out_o", [TC, D], F32)

    g = c.g
    rg = [list(range(NCORE))]

    def ag_issue(src, dst, name, gates=()):
        cc = c.sem(name)
        for nm, val in gates:
            g.wait_ge(c._sems[nm], val)
        g.collective_compute("AllGather", mybir.AluOpType.bypass,
                             replica_groups=rg,
                             ins=[src[:].opt()], outs=[dst[:].opt()]
                             ).then_inc(cc, 1)
        c.cnt[name] += 1
        return cc

    io["_ag"] = ag_issue

    c.load_idents()
    phase_A(c, io)
    c.barrier()
    g.wait_ge(c._sems["cc0"], 1)
    g.wait_ge(c._sems["cc1"], 1)
    phase_B(c, io)
    c.barrier()
    phase_C(c, io)
    c.barrier()
    g.wait_ge(c._sems["cc2"], 1)
    g.wait_ge(c._sems["cc3"], 1)
    c.sy.wait_ge(c._sems["cc3"], 1)
    phase_D(c, io)
    c.barrier()
    cc4 = ag_issue(io["spec_d"], io["spec_g"], "cc4")
    g.wait_ge(cc4, 1)
    phase_E(c, io)
    return c


def host_consts():
    ident = np.eye(128, dtype=np.float32)
    return {"identbf": ident.astype(ml_dtypes.bfloat16), "identf32": ident}


def _fused():
    if "F" not in _CACHE:
        _CACHE["F"] = build_fused()
    return _CACHE["F"]


def _dropout_mask():
    if "M" not in _CACHE:
        import jax
        with jax.default_device(jax.local_devices(backend="cpu")[0]):
            key = jax.random.key(42, impl="threefry2x32")
            keep = np.asarray(jax.random.bernoulli(key, 0.8,
                                                   (1, T, D))).reshape(T, D)
        _CACHE["M"] = (keep * 1.25).astype(ml_dtypes.bfloat16)
    return _CACHE["M"]


def kernel(**inputs):
    ins = {k: np.asarray(v) for k, v in inputs.items()}
    c = _fused()
    con = host_consts()
    fc = fft_consts()
    x = ins["L0_attn_x"].reshape(T, D)
    M = _dropout_mask()
    ct = lambda a: np.ascontiguousarray(a)
    P = "L1_attn_"

    sel_host = np.zeros((8, 8 * 128), dtype=ml_dtypes.bfloat16)
    for e in range(8):
        sel_host[e, e * 128:(e + 1) * 128] = 1

    shared = {
        "w1": ins["L0_attn_W1"], "b1": ins["L0_attn_b1"],
        "w2": ins["L0_attn_W2"], "b2": ins["L0_attn_b2"],
        "w3": ins["L0_attn_W3"], "b3": ins["L0_attn_b3"],
        "w4": ins["L0_attn_W4"], "b4": ins["L0_attn_b4"].reshape(1, D),
        "fw1t": ct(ins["L0_ffn_W1"].T), "fw2t": ct(ins["L0_ffn_W2"].T),
        "fb1": ins["L0_ffn_b1"], "fb2": ins["L0_ffn_b2"],
        "norm_g": ins[P + "norm_g"], "norm_b": ins[P + "norm_b"],
        "wct": ct(ins[P + "Wconcept"].T), "bcon": ins[P + "bconcept"],
        "wcgt": ct(ins[P + "Wcg"].T), "bcg": ins[P + "bcg"],
        "wgt": ct(ins[P + "Wgate"].T), "bgate": ins[P + "bgate"],
        "wspt": ct(ins[P + "Wsp"].T),
        "wfgt": ct(ins[P + "Wfg"].T), "bfg": ins[P + "bfg"],
        "wtgt": ct(ins[P + "Wtg"].T), "btg": ins[P + "btg"].reshape(1, 1),
        "wbandt": ct(ins[P + "Wband"].T), "bband": ins[P + "bband"].reshape(1, 4),
        "wgmoet": ct(ins["L1_ffn_Wg"].T),
        "lnsc": ins["L1_ffn_ln_scale"], "lnbi": ins["L1_ffn_ln_bias"],
        "w1e": ins["L1_ffn_W1"], "b1e": ins["L1_ffn_b1"],
        "w2e": ins["L1_ffn_W2"], "b2e": ins["L1_ffn_b2"],
        "selrep": sel_host,
        "fi_full": ins[P + "freq_importance"].reshape(1, -1),
        "wfst": ct(ins[P + "Wfs"].T),
        "oscal": ins[P + "output_scale"].reshape(1, 1),
        "n2g": ins[P + "norm2_g"], "n2b": ins[P + "norm2_b"],
        "w1t": ct(ins[P + "W1"].T), "eb1": ins[P + "b1"],
        "w2t": ct(ins[P + "W2"].T), "eb2": ins[P + "b2"],
        **con, **fc,
    }

    p_ = np.arange(128)
    a_ = np.arange(8)
    per = []
    for ci in range(NCORE):
        sl = slice(ci * TC, (ci + 1) * TC)
        c0 = ci * CPC
        cc = slice(c0, c0 + CPC)
        uidx = (a_[None, :] * D + np.where(p_[:, None] >= 64, NF, 0)
                + c0 + (p_[:, None] % 64)).astype(np.int32)
        row = (2 * (a_[None, :] % 4) + p_[:, None] // 64) * 128 \
            + 64 * (a_[None, :] // 4) + (p_[:, None] % 64)
        sidx = (row * NCORE + ci).astype(np.int32)
        bsel = np.zeros((128, 4), dtype=np.float32)
        bsel[:, ci // 2] = 1.0
        per.append({
            "x": ct(x[sl]), "mtm": ct(M[sl]),
            "ld_c": ins[P + "log_decay"][cc].reshape(1, -1),
            "fi_c": ins[P + "freq_importance"][cc].reshape(1, -1),
            "fr_c": ins[P + "frequencies"][cc].reshape(1, -1),
            "dt_c": ins[P + "dt"][cc].reshape(1, -1),
            "uidx": uidx, "sidx": sidx, "bsel": bsel,
            **shared,
        })
    res = run_bass_kernel_spmd(c.nc, per, core_ids=list(range(NCORE))).results
    out = np.concatenate([np.asarray(r["out_o"]) for r in res], axis=0)
    return out.reshape(1, T, D)
